# version 01
# baseline (speedup 1.0000x reference)
import math
from contextlib import ExitStack

import numpy as np

import concourse.bass as bass
import concourse.mybir as mybir
from concourse.bass_utils import run_bass_kernel_spmd

F32 = mybir.dt.float32
BF16 = mybir.dt.bfloat16
AF = mybir.ActivationFunctionType
ALU = mybir.AluOpType
AX = mybir.AxisListType

NC = 8
NLOC = 1024
D = 2048
DT = 16
L = 1024
H = 3
HID = [1024, 2048, 3072, 4096, 1024, 2048, 3072, 4096]
SCALE = 1.0 / math.sqrt(2048.0)
EXP_BIAS = -4.0
EPS = 1e-5


def _sel8_np():
    a = np.zeros((8, 1024), dtype=np.float32)
    for e in range(8):
        a[e, e * 128:(e + 1) * 128] = 1.0
    return a


def build(upto=9, tap=None):
    nc = bass.Bass("TRN2", target_bir_lowering=False)

    def din(name, shape):
        return nc.dram_tensor(name, list(shape), F32, kind="ExternalInput").ap()

    xT = din("xT", (D, NLOC))
    wqT = din("wqT", (H, D, D))
    wkT = din("wkT", (H, D, D))
    wvT = din("wvT", (H, D, D))
    fcwT = din("fcwT", (D, D))
    fc_b = din("fc_b", (D,))
    down_w = din("down_w", (D, L))
    down_b = din("down_b", (L,))
    up_w = din("up_w", (L, D))
    up_b = din("up_b", (D,))
    router_w = din("router_w", (D, 8))
    router_b = din("router_b", (8,))
    sh_w1 = din("sh_w1", (L, L))
    sh_b1 = din("sh_b1", (L,))
    sh_w2 = din("sh_w2", (L, L))
    sh_b2 = din("sh_b2", (L,))
    core_w = din("core_w", (D, D))
    core_b = din("core_b", (D,))
    ew1 = [din(f"ew1_{k}", (L, HID[k])) for k in range(8)]
    eb1 = [din(f"eb1_{k}", (HID[k],)) for k in range(8)]
    ew2 = [din(f"ew2_{k}", (HID[k], L)) for k in range(8)]
    eb2 = [din(f"eb2_{k}", (L,)) for k in range(8)]

    outT = nc.dram_tensor("outT", [D, NLOC], F32, kind="ExternalOutput").ap()

    k_in = nc.dram_tensor("k_in", [H * D, NLOC], BF16, kind="Internal").ap()
    k_g = nc.dram_tensor("k_g", [NC * H * D, NLOC], BF16, kind="Internal").ap()
    v_in = nc.dram_tensor("v_in", [H * 4 * NLOC, 512], BF16, kind="Internal").ap()
    v_g = nc.dram_tensor("v_g", [NC * H * 4 * NLOC, 512], BF16, kind="Internal").ap()
    s_in = nc.dram_tensor("s_in", [128, 192], F32, kind="Internal").ap()
    s_red = nc.dram_tensor("s_red", [128, 192], F32, kind="Internal").ap()
    es = nc.dram_tensor("es", [H * NC * NLOC, NLOC], BF16, kind="Internal").ap()
    fsp = nc.dram_tensor("fsp", [DT, 128, NLOC], F32, kind="Internal").ap()
    x2sp = nc.dram_tensor("x2sp", [DT, 128, NLOC], F32, kind="Internal").ap()

    ident = nc.inline_tensor(np.eye(128, dtype=np.float32), "c_ident").ap()
    sel8 = nc.inline_tensor(_sel8_np(), "c_sel8").ap()
    c_onescol = nc.inline_tensor(np.ones((128, 1), dtype=np.float32), "c_onescol").ap()
    c_onesrow = nc.inline_tensor(np.ones((1, 128), dtype=np.float32), "c_onesrow").ap()
    c_ones8 = nc.inline_tensor(np.ones((8, 1), dtype=np.float32), "c_ones8").ap()
    c_neg8 = nc.inline_tensor(np.full((8, 1), -1.0, dtype=np.float32), "c_neg8").ap()
    cst_np = np.zeros((128, 4), dtype=np.float32)
    cst_np[:, 0] = EXP_BIAS
    cst_np[:, 1] = EPS
    cst_np[:, 2] = 1.0
    c_cst = nc.inline_tensor(cst_np, "c_cst").ap()
    lam = (np.float32(1e-7) * np.array([2048.0 * h for h in HID], dtype=np.float32))
    c_lam = nc.inline_tensor(lam.reshape(1, 8).astype(np.float32), "c_lam").ap()

    with ExitStack() as ctx:
        E = ctx.enter_context

        xn1 = E(nc.sbuf_tensor("b_xn1", [128, DT, NLOC], BF16))
        qsb = E(nc.sbuf_tensor("b_q", [128, DT, NLOC], BF16))
        bw0 = E(nc.sbuf_tensor("b_bw0", [128, 4096], BF16))
        bw1 = E(nc.sbuf_tensor("b_bw1", [128, 4096], BF16))
        sw0 = E(nc.sbuf_tensor("b_sw0", [128, 4096], BF16))
        sw1 = E(nc.sbuf_tensor("b_sw1", [128, 4096], BF16))
        esb = E(nc.sbuf_tensor("b_es", [128, 8192], BF16))
        moe = E(nc.sbuf_tensor("b_moe", [128, 8, NLOC], F32))
        xst = E(nc.sbuf_tensor("b_xst", [128, 2, NLOC], F32))
        rst = E(nc.sbuf_tensor("b_rst", [128, 2, NLOC], F32))
        sqb = E(nc.sbuf_tensor("b_sq", [128, NLOC], F32))
        gtmp = E(nc.sbuf_tensor("b_gtmp", [128, 2, 1024], BF16))
        ga = [E(nc.sbuf_tensor(f"b_ga{i}", [128, 4, 256], BF16)) for i in range(2)]
        bwv = E(nc.sbuf_tensor("b_bwv", [128, 1024], BF16))
        row0 = E(nc.sbuf_tensor("b_row0", [1, 1024], F32))
        row1 = E(nc.sbuf_tensor("b_row1", [1, 1024], F32))
        row2 = E(nc.sbuf_tensor("b_row2", [1, 1024], F32))
        wTr = E(nc.sbuf_tensor("b_wT", [8, 1024], F32))
        gR = E(nc.sbuf_tensor("b_gR", [8, 1024], F32))
        b2st = E(nc.sbuf_tensor("b_b2st", [8, 1024], F32))
        ssb = E(nc.sbuf_tensor("b_s", [128, 192], F32))
        isb = E(nc.sbuf_tensor("b_is", [128, 192], F32))
        colb = E(nc.sbuf_tensor("b_colb", [128, 56], F32))
        gb1c = E(nc.sbuf_tensor("b_gb1", [128, 32], BF16))
        b1cf = E(nc.sbuf_tensor("b_b1c", [128, 32], F32))
        rsm = E(nc.sbuf_tensor("b_rsm", [128, 64], F32))
        rcol = E(nc.sbuf_tensor("b_rcol", [128, 8], F32))
        wsb = E(nc.sbuf_tensor("b_wsb", [128, 8, 8], F32))
        rtw = E(nc.sbuf_tensor("b_rtw", [128, DT, 8], F32))
        idt = E(nc.sbuf_tensor("b_idt", [128, 128], F32))
        ocol = E(nc.sbuf_tensor("b_ocol", [128, 1], F32))
        orow = E(nc.sbuf_tensor("b_orow", [1, 128], F32))
        o8 = E(nc.sbuf_tensor("b_o8", [8, 1], F32))
        n8 = E(nc.sbuf_tensor("b_n8", [8, 1], F32))
        cstb = E(nc.sbuf_tensor("b_cstb", [128, 4], F32))
        lamb = E(nc.sbuf_tensor("b_lamb", [1, 8], F32))
        brow = E(nc.sbuf_tensor("b_brow", [1, 8], F32))
        crw = E(nc.sbuf_tensor("b_crw", [1, 8], F32))
        sb2c = E(nc.sbuf_tensor("b_sb2c", [128, 8], F32))
        browb = E(nc.sbuf_tensor("b_browb", [128, 8], F32))
        bstg = E(nc.sbuf_tensor("b_bstg", [32, 128], F32))

        PA = [E(nc.psum_tensor(f"p_a{i}", [128, NLOC], F32)) for i in range(4)]

        SEM = {}
        for s in ("lsp", "lgp", "te", "ve", "sc", "ck", "cs", "cv", "outp"):
            SEM[s] = E(nc.semaphore("sem_" + s))

        block = E(nc.Block())

        wsl = [bw0.ap().rearrange("p (t d) -> p t d", d=512),
               bw1.ap().rearrange("p (t d) -> p t d", d=512)]
        es_k = esb.ap().rearrange("p (t m) -> p t m", m=1024)
        es_v = esb.ap().rearrange("p (q t m) -> p q t m", q=2, m=512)
        sw_v = [sw0.ap().rearrange("p (t m) -> p t m", m=512),
                sw1.ap().rearrange("p (t m) -> p t m", m=512)]
        es_st = esb.ap().rearrange("p (s t m) -> p s t m", s=2, m=1024)
        kpc = [wsl[0][:, 0:4, :], wsl[1][:, 0:4, :]]
        ppc = [sw0.ap().rearrange("p (t m) -> p t m", m=1024),
               sw1.ap().rearrange("p (t m) -> p t m", m=1024)]
        w1sl = ppc
        w2sl = [bw0.ap().rearrange("p (t m) -> p t m", m=1024),
                bw1.ap().rearrange("p (t m) -> p t m", m=1024)]
        upsl = [sw_v[0][:, 0:4, :], sw_v[1][:, 0:4, :]]
        cosl = wsl
        moebf = esb.ap().rearrange("p (t m) -> p t m", m=1024)
        xr = xT.rearrange("(t p) n -> t p n", p=128)

        def program(who):
            EN = {"sp": nc.sync, "te": nc.tensor, "ve": nc.vector,
                  "sc": nc.scalar, "gp": nc.gpsimd}
            C = {s: 0 for s in SEM}
            fresh = {}
            rel = {"b0": 0, "b1": 0, "s0": 0, "s1": 0}

            class SV:
                def __init__(self, pfx):
                    self.p = pfx

                def __getitem__(self, i):
                    return rel[self.p + str(i)]

                def __setitem__(self, i, v):
                    rel[self.p + str(i)] = v

            class PV:
                def __getitem__(self, i):
                    return max(rel["b" + str(i)], rel["s" + str(i)])

                def __setitem__(self, i, v):
                    rel["b" + str(i)] = v
                    rel["s" + str(i)] = v

            evc = [0]

            def emit(e, builder, sem=None, n=1):
                ins = builder(EN[e]) if who == e else None
                if sem is not None:
                    C[sem] += n
                    if ins is not None:
                        ins.then_inc(SEM[sem], n)
                return C[sem] if sem is not None else None

            def wge(e, sem, val=None):
                v = C[sem] if val is None else val
                if v > 0 and who == e:
                    EN[e].wait_ge(SEM[sem], v)

            def dma(e, out, in_, **kw):
                q = "lsp" if e == "sp" else "lgp"
                return emit(e, lambda h: h.dma_start(out=out, in_=in_, **kw), q, 16)

            def mm(pi, out_ap, lhsT, rhs, col=0, sem=None, stop=False):
                key = (pi, col // 512)
                st = fresh.get(key, True)
                fresh[key] = False
                return emit("te", lambda h: h.matmul(out_ap, lhsT, rhs, start=st,
                                                     stop=stop, skip_group_check=True),
                            sem, 1)

            def release(pi, col=None):
                if col is None:
                    fresh[(pi, 0)] = True
                    fresh[(pi, 1)] = True
                else:
                    fresh[(pi, col // 512)] = True

            def vop(builder, sync=True):
                v = emit("ve", builder, "ve", 1)
                if sync:
                    wge("ve", "ve", v)
                return v

            def sop(builder, sync=True):
                v = emit("sc", builder, "sc", 1)
                if sync:
                    wge("sc", "sc", v)
                return v

            def load_cols(dst, vec, nt, pa=3):
                wge("gp", "te", C["te"])
                vstg = dma("gp", bstg[0:nt, :], vec.rearrange("(t p) -> t p", p=128))
                wge("te", "lgp", vstg)
                v = mm(pa, PA[pa][:, 0:nt], bstg[0:nt, :], idt[0:nt, 0:nt],
                       col=0, sem="te", stop=True)
                wge("ve", "te", v)
                vc = vop(lambda h: h.tensor_copy(dst, PA[pa][:, 0:nt]), sync=False)
                release(pa)
                return vc

            dma("sp", idt[:], ident)
            dma("sp", ocol[:], c_onescol)
            dma("sp", orow[:], c_onesrow)
            dma("sp", o8[:], c_ones8)
            dma("sp", n8[:], c_neg8)
            dma("sp", cstb[:], c_cst)
            dma("sp", lamb[:], c_lam)
            dma("sp", brow[:], router_b[None, :])
            dma("sp", rtw.ap(), router_w.rearrange("(t p) e -> p t e", p=128))
            for k in range(8):
                dma("sp", b2st[k:k + 1, :], eb2[k][None, :])
            v_const = C["lsp"]
            wge("te", "lsp", v_const)
            load_cols(colb[:, 0:16], fc_b, 16)
            load_cols(colb[:, 16:24], down_b, 8)
            load_cols(colb[:, 24:40], up_b, 16)
            load_cols(colb[:, 40:56], core_b, 16)
            load_cols(sb2c[:], sh_b2, 8)
            v_cols = C["ve"]
            wge("ve", "lsp", v_const)
            wge("ve", "ve", v_cols)
            vop(lambda h: h.tensor_tensor(colb[:, 24:40], colb[:, 24:40],
                                          colb[:, 40:56], ALU.add))
            vop(lambda h: h.tensor_scalar(sb2c[:], sb2c[:], 0.1, None, ALU.mult))
            v_brow = vop(lambda h: h.tensor_tensor(brow[:], brow[:], lamb[:],
                                                   ALU.subtract))

            cons = [0, 0]
            v_sq_cons = 0
            for t in range(DT):
                sl = t % 2
                wge("sp", "te", cons[sl])
                vx = dma("sp", xst[:, sl, :], xr[t])
                wge("sc", "lsp", vx)
                wge("sc", "te", v_sq_cons)
                vs = emit("sc", lambda h, s=sl: h.activation(sqb[:], xst[:, s, :],
                                                             AF.Square), "sc", 1)
                wge("te", "lsp", vx)
                mm(2, PA[2][0:1, 0:512], ocol[:], xst[:, sl, 0:512], col=0)
                mm(2, PA[2][0:1, 512:1024], ocol[:], xst[:, sl, 512:1024], col=512)
                wge("te", "sc", vs)
                mm(3, PA[3][0:1, 0:512], ocol[:], sqb[:, 0:512], col=0)
                cons[sl] = mm(3, PA[3][0:1, 512:1024], ocol[:], sqb[:, 512:1024],
                              col=512, sem="te", stop=(t == DT - 1))
                v_sq_cons = cons[sl]
            v_stats = C["te"]

            def ln_rows(v_te_done):
                wge("ve", "te", v_te_done)
                vop(lambda h: h.tensor_copy(row0[:], PA[2][0:1, :]))
                vop(lambda h: h.tensor_copy(row1[:], PA[3][0:1, :]))
                release(2)
                release(3)
                vop(lambda h: h.tensor_scalar(row0[:], row0[:], 1.0 / 2048.0, None,
                                              ALU.mult))
                vop(lambda h: h.tensor_scalar(row1[:], row1[:], 1.0 / 2048.0, None,
                                              ALU.mult))
                vop(lambda h: h.tensor_tensor(row2[:], row0[:], row0[:], ALU.mult))
                vv = vop(lambda h: h.tensor_tensor(row1[:], row1[:], row2[:],
                                                   ALU.subtract))
                wge("sc", "ve", vv)
                vs = sop(lambda h: h.activation(row1[:], row1[:], AF.Sqrt,
                                                bias=cstb[0:1, 1:2]))
                wge("ve", "sc", vs)
                vop(lambda h: h.reciprocal(row1[:], row1[:]))
                vr = vop(lambda h: h.tensor_tensor(row2[:], row0[:], row1[:], ALU.mult))
                wge("te", "ve", vr)
                wge("te", "sc", C["sc"])
                release(0)
                release(1)
                mm(0, PA[0][:, 0:512], orow[:], row1[0:1, 0:512], col=0)
                mm(0, PA[0][:, 512:1024], orow[:], row1[0:1, 512:1024], col=512)
                mm(1, PA[1][:, 0:512], orow[:], row2[0:1, 0:512], col=0)
                vb = mm(1, PA[1][:, 512:1024], orow[:], row2[0:1, 512:1024],
                        col=512, sem="te", stop=True)
                return vb

            vb1 = ln_rows(v_stats)
            cons = [0, 0]
            for t in range(DT):
                sl = t % 2
                wge("sp", "ve", cons[sl])
                vx = dma("sp", xst[:, sl, :], xr[t])
                wge("ve", "lsp", vx)
                wge("ve", "te", vb1)
                emit("ve", lambda h, s=sl: h.tensor_tensor(sqb[:], xst[:, s, :],
                                                           PA[0][:], ALU.mult))
                cons[sl] = emit("ve", lambda h, t_=t: h.tensor_tensor(
                    xn1[:, t_, :], sqb[:], PA[1][:], ALU.subtract), "ve", 1)
            v_xn1 = C["ve"]
            release(0)
            release(1)

            if upto == 1:
                wge("gp", "ve", v_xn1)
                emit("gp", lambda h: h.dma_start(
                    out=outT.rearrange("(t p) n -> p t n", p=128), in_=xn1.ap()),
                    "outp", 16)
                wge("sp", "outp", 16)
                return

            def wpieces(wap):
                out = []
                for db in range(4):
                    for eh in range(2):
                        src = wap[eh * 1024:(eh + 1) * 1024, db * 512:(db + 1) * 512]
                        out.append(src.rearrange("(t p) d -> p t d", p=128))
                return out

            wge("te", "ve", v_xn1)
            wrel = SV("b")
            swrel = SV("s")
            st_rel = 0

            def proj_K(hh):
                nonlocal st_rel
                pieces = wpieces(wkT[hh])
                vev = evc[0]
                for dh in range(2):
                    for db in (2 * dh, 2 * dh + 1):
                        for eh in range(2):
                            k = db * 2 + eh
                            sl = k % 2
                            wge("gp", "te", wrel[sl])
                            vw = dma("gp", wsl[sl], pieces[k])
                            wge("te", "lgp", vw)
                            if eh == 0:
                                wge("te", "ve", vev)
                            for dtr in range(4):
                                for et in range(8):
                                    for nh in range(2):
                                        is_last = (dtr == 3 and et == 7 and nh == 1)
                                        wrel[sl] = mm(
                                            dtr, PA[dtr][:, nh * 512:(nh + 1) * 512],
                                            wsl[sl][:, et, dtr * 128:(dtr + 1) * 128],
                                            xn1[:, eh * 8 + et, nh * 512:(nh + 1) * 512],
                                            col=nh * 512,
                                            sem="te" if is_last else None)
                        vte = C["te"]
                        wge("ve", "te", vte)
                        wge("ve", "lsp", st_rel)
                        for dtr in range(4):
                            vev = emit("ve", lambda h, d=dtr, b=db: h.tensor_copy(
                                es_k[:, (b % 2) * 4 + d, :], PA[d][:]),
                                "ve" if dtr == 3 else None, 1)
                            release(dtr)
                        evc[0] = vev
                    wge("sp", "ve", vev)
                    dst = k_in[hh * D + dh * 1024: hh * D + (dh + 1) * 1024, :]
                    st_rel = dma("sp", dst.rearrange("(t p) m -> p t m", p=128), es_k)

            for hh in range(H):
                proj_K(hh)

            def proj_V(hh):
                nonlocal st_rel
                pieces = wpieces(wvT[hh])
                vev = evc[0]
                for db in range(4):
                    for eh in range(2):
                        k = db * 2 + eh
                        sl = k % 2
                        wge("gp", "te", wrel[sl])
                        vw = dma("gp", wsl[sl], pieces[k])
                        wge("te", "lgp", vw)
                        if eh == 0:
                            wge("te", "ve", vev)
                        for mt in range(8):
                            pi = mt // 2
                            colo = (mt % 2) * 512
                            for et in range(8):
                                is_last = (mt == 7 and et == 7)
                                wrel[sl] = mm(
                                    pi, PA[pi][:, colo:colo + 512],
                                    xn1[:, eh * 8 + et, mt * 128:(mt + 1) * 128],
                                    wsl[sl][:, et, :], col=colo,
                                    sem="te" if is_last else None)
                    vte = C["te"]
                    wge("ve", "te", vte)
                    if db == 0:
                        wge("ve", "lsp", st_rel)
                    for mt in range(8):
                        pi = mt // 2
                        colo = (mt % 2) * 512
                        sm = "ve" if mt == 7 else None
                        if db < 2:
                            vev = emit("ve", lambda h, p=pi, c=colo, qq=db, m=mt:
                                       h.tensor_copy(es_v[:, qq, m, :],
                                                     PA[p][:, c:c + 512]), sm, 1)
                        else:
                            vev = emit("ve", lambda h, p=pi, c=colo, qq=db, m=mt:
                                       h.tensor_copy(sw_v[qq - 2][:, m, :],
                                                     PA[p][:, c:c + 512]), sm, 1)
                        release(pi, colo)
                    evc[0] = vev
                wge("sp", "ve", vev)
                for q in range(4):
                    dst = v_in[(hh * 4 + q) * NLOC:(hh * 4 + q + 1) * NLOC, :]
                    src = es_v[:, q, :, :] if q < 2 else sw_v[q - 2]
                    st_rel = dma("sp", dst.rearrange("(t p) m -> p t m", p=128), src)

            for hh in range(H):
                proj_V(hh)
            wge("gp", "lsp", C["lsp"])
            wge("gp", "lgp", C["lgp"])
            emit("gp", lambda h: h.collective_compute(
                "AllGather", ALU.bypass, replica_groups=[list(range(NC))],
                ins=[k_in], outs=[k_g]), "ck", 1)
            wge("gp", "ck", 1)
            emit("gp", lambda h: h.collective_compute(
                "AllGather", ALU.bypass, replica_groups=[list(range(NC))],
                ins=[v_in], outs=[v_g]), "cv", 1)
            wge("gp", "cv", 1)

            if upto == 2:
                wge("gp", "lsp", st_rel)
                if tap == "k":
                    src = k_in[0:D, :]
                else:
                    src = v_in.rearrange("(a b) m -> a (b m)", b=2)[0:D, 0:NLOC]
                emit("gp", lambda h: h.dma_start(
                    out=outT.rearrange("(t p) n -> p t n", p=128),
                    in_=src.rearrange("(t p) n -> p t n", p=128)), "outp", 16)
                wge("sp", "outp", 16)
                return

            wge("sp", "ck", 1)
            wge("gp", "cv", 1)
            kq_rel = SV("b")
            st_store = [0, 0]
            v_exp = 0
            for hh in range(H):
                pieces = wpieces(wqT[hh])
                qrel = evc[0]
                wge("te", "sc", v_exp)
                for db in range(4):
                    for eh in range(2):
                        k = db * 2 + eh
                        sl = k % 2
                        wge("gp", "te", wrel[sl])
                        vw = dma("gp", wsl[sl], pieces[k])
                        wge("te", "lgp", vw)
                        if eh == 0:
                            wge("te", "ve", qrel)
                        for dtr in range(4):
                            for et in range(8):
                                for nh in range(2):
                                    is_last = (dtr == 3 and et == 7 and nh == 1)
                                    wrel[sl] = mm(
                                        dtr, PA[dtr][:, nh * 512:(nh + 1) * 512],
                                        wsl[sl][:, et, dtr * 128:(dtr + 1) * 128],
                                        xn1[:, eh * 8 + et, nh * 512:(nh + 1) * 512],
                                        col=nh * 512,
                                        sem="te" if is_last else None)
                    vte = C["te"]
                    wge("ve", "te", vte)
                    for dtr in range(4):
                        qrel = emit("ve", lambda h, d=dtr, b=db: h.tensor_copy(
                            qsb[:, b * 4 + d, :], PA[d][:]),
                            "ve" if dtr == 3 else None, 1)
                        release(dtr)
                    evc[0] = qrel
                wge("te", "ve", qrel)

                for r in range(NC):
                    for mh in range(2):
                        chunk = (hh * NC + r) * 2 + mh
                        base = r * (H * D) + hh * D
                        for dq in range(4):
                            sl2 = dq % 2
                            wge("sp", "te", kq_rel[sl2])
                            src = k_g[base + dq * 512: base + (dq + 1) * 512,
                                      mh * 512:(mh + 1) * 512]
                            vw = dma("sp", kpc[sl2],
                                     src.rearrange("(t p) m -> p t m", p=128))
                            wge("te", "lsp", vw)
                            if dq == 0:
                                wge("te", "sc", v_exp)
                            for mtr in range(4):
                                for dtr in range(4):
                                    dtg = dq * 4 + dtr
                                    for nh in range(2):
                                        is_last = (mtr == 3 and dtr == 3 and nh == 1)
                                        kq_rel[sl2] = mm(
                                            mtr, PA[mtr][:, nh * 512:(nh + 1) * 512],
                                            kpc[sl2][:, dtr, mtr * 128:(mtr + 1) * 128],
                                            qsb[:, dtg, nh * 512:(nh + 1) * 512],
                                            col=nh * 512,
                                            sem="te" if is_last else None)
                        vte = C["te"]
                        ssl = chunk % 2
                        wge("sc", "te", vte)
                        wge("sc", "lsp", st_store[ssl])
                        for mtr in range(4):
                            coli = hh * 64 + r * 8 + mh * 4 + mtr
                            v_exp = emit("sc", lambda h, m=mtr, c2=coli, s=ssl:
                                         h.activation(es_st[:, s, m, :], PA[m][:],
                                                      AF.Exp, bias=cstb[:, 0:1],
                                                      scale=SCALE,
                                                      accum_out=ssb[:, c2:c2 + 1]),
                                         "sc" if mtr == 3 else None, 1)
                            release(mtr)
                        wge("sp", "sc", v_exp)
                        ro = hh * NC * NLOC + r * NLOC + mh * 512
                        st_store[ssl] = dma(
                            "sp", es[ro:ro + 512, :].rearrange("(t p) n -> p t n", p=128),
                            es_st[:, ssl, :, :])
            v_s_done = C["sc"]

            wge("gp", "sc", v_s_done)
            wge("gp", "lsp", C["lsp"])
            v1 = dma("gp", s_in, ssb[:])
            wge("gp", "lgp", v1)
            emit("gp", lambda h: h.collective_compute(
                "AllReduce", ALU.add, replica_groups=[list(range(NC))],
                ins=[s_in], outs=[s_red]), "cs", 1)
            wge("gp", "cs", 1)
            v2 = dma("gp", ssb[:], s_red)
            wge("ve", "lgp", v2)
            vop(lambda h: h.tensor_scalar(isb[:], ssb[:], 3.0, None, ALU.mult))
            v_inv = vop(lambda h: h.reciprocal(isb[:], isb[:]))

            if upto == 3:
                wge("gp", "ve", v_inv)
                if tap == "es":
                    emit("gp", lambda h: h.dma_start(
                        out=outT.rearrange("(t p) n -> p t n", p=128),
                        in_=es[0:2048, :].rearrange("(t p) n -> p t n", p=128)),
                        "outp", 16)
                else:
                    emit("gp", lambda h: h.dma_start(out=outT[0:128, 0:192],
                                                     in_=ssb[:]), "outp", 16)
                wge("sp", "outp", 16)
                return

            wge("sp", "cv", 1)
            wge("te", "sc", v_exp)
            p_rel = PV()
            for q in range(4):
                idx = 0
                for r in range(NC):
                    for hh in range(H):
                        for mh in range(2):
                            sl = idx % 2
                            idx += 1
                            wge("sp", "te", p_rel[sl])
                            ro = hh * NC * NLOC + r * NLOC + mh * 512
                            vp = dma("sp", ppc[sl],
                                     es[ro:ro + 512, :].rearrange(
                                         "(t p) n -> p t n", p=128))
                            vo2 = r * (H * 4 * NLOC) + (hh * 4 + q) * NLOC + mh * 512
                            vv = dma("sp", kpc[sl],
                                     v_g[vo2:vo2 + 512, :].rearrange(
                                         "(t p) m -> p t m", p=128))
                            wge("ve", "lsp", vv)
                            for mtr in range(4):
                                ci = hh * 64 + r * 8 + mh * 4 + mtr
                                vsc2 = emit("ve", lambda h, m=mtr, c2=ci, s=sl:
                                            h.tensor_scalar(kpc[s][:, m, :],
                                                            kpc[s][:, m, :],
                                                            isb[:, c2:c2 + 1],
                                                            None, ALU.mult),
                                            "ve" if mtr == 3 else None, 1)
                            wge("te", "lsp", vp)
                            wge("te", "ve", vsc2)
                            for mtr in range(4):
                                for dtr in range(4):
                                    for nh in range(2):
                                        is_last = (mtr == 3 and dtr == 3 and nh == 1)
                                        p_rel[sl] = mm(
                                            dtr, PA[dtr][:, nh * 512:(nh + 1) * 512],
                                            kpc[sl][:, mtr, dtr * 128:(dtr + 1) * 128],
                                            ppc[sl][:, mtr, nh * 512:(nh + 1) * 512],
                                            col=nh * 512,
                                            sem="te" if is_last else None)
                vte = C["te"]
                wge("ve", "te", vte)
                for dtr in range(4):
                    vev = emit("ve", lambda h, d=dtr, qq=q: h.tensor_copy(
                        qsb[:, qq * 4 + d, :], PA[d][:]),
                        "ve" if dtr == 3 else None, 1)
                    release(dtr)
                wge("te", "ve", vev)
            v_attn = C["ve"]

            if upto == 5:
                wge("gp", "ve", v_attn)
                emit("gp", lambda h: h.dma_start(
                    out=outT.rearrange("(t p) n -> p t n", p=128), in_=qsb.ap()),
                    "outp", 16)
                wge("sp", "outp", 16)
                return

            wge("te", "ve", v_attn)
            pcons = [0, 0]
            scons = [0, 0]
            fcons = [0, 0]
            for jb in range(4):
                for dh in range(2):
                    wge("gp", "te", wrel[dh])
                    src = fcwT[dh * 1024:(dh + 1) * 1024, jb * 512:(jb + 1) * 512]
                    dma("gp", wsl[dh], src.rearrange("(t p) d -> p t d", p=128))
                v_w = C["lgp"]
                wge("te", "lgp", v_w)
                for jtr in range(4):
                    jt = jb * 4 + jtr
                    pi = jtr % 2
                    wge("te", "sc", pcons[pi])
                    for dh in range(2):
                        for dt8 in range(8):
                            for nh in range(2):
                                is_last = (dh == 1 and dt8 == 7 and nh == 1)
                                wrel[dh] = mm(
                                    pi, PA[pi][:, nh * 512:(nh + 1) * 512],
                                    wsl[dh][:, dt8, jtr * 128:(jtr + 1) * 128],
                                    qsb[:, dh * 8 + dt8, nh * 512:(nh + 1) * 512],
                                    col=nh * 512, sem="te" if is_last else None)
                    vte = C["te"]
                    sl2 = jt % 2
                    wge("sc", "te", max(vte, v_sq_cons, scons[sl2]))
                    wge("sc", "lsp", fcons[sl2])
                    emit("sc", lambda h, p=pi, s=sl2, j=jt: h.activation(
                        xst[:, s, :], PA[p][:], AF.Identity, bias=colb[:, j:j + 1]))
                    pcons[pi] = emit("sc", lambda h, s=sl2: h.activation(
                        sqb[:], xst[:, s, :], AF.Square), "sc", 1)
                    release(pi)
                    wge("te", "sc", pcons[pi])
                    mm(2, PA[2][0:1, 0:512], ocol[:], xst[:, sl2, 0:512], col=0)
                    mm(2, PA[2][0:1, 512:1024], ocol[:], xst[:, sl2, 512:1024], col=512)
                    mm(3, PA[3][0:1, 0:512], ocol[:], sqb[:, 0:512], col=0)
                    scons[sl2] = mm(3, PA[3][0:1, 512:1024], ocol[:], sqb[:, 512:1024],
                                    col=512, sem="te", stop=(jt == DT - 1))
                    v_sq_cons = scons[sl2]
                    wge("sp", "sc", pcons[pi])
                    fcons[sl2] = dma("sp", fsp[jt], xst[:, sl2, :])
            vb3 = ln_rows(C["te"])
            vcons = [0, 0]
            tcons = [0, 0]
            for t in range(DT):
                sl = t % 2
                wge("sp", "ve", vcons[sl])
                wge("sp", "te", tcons[sl])
                dma("sp", xst[:, sl, :], fsp[t])
                vxr = dma("sp", rst[:, sl, :], xr[t])
                wge("ve", "lsp", vxr)
                wge("ve", "te", max(vb3, v_sq_cons))
                emit("ve", lambda h, s=sl: h.tensor_tensor(sqb[:], xst[:, s, :],
                                                           PA[0][:], ALU.mult))
                emit("ve", lambda h: h.tensor_tensor(sqb[:], sqb[:], PA[1][:],
                                                     ALU.subtract))
                vx2 = emit("ve", lambda h, s=sl: h.tensor_tensor(
                    rst[:, s, :], sqb[:], rst[:, s, :], ALU.add), "ve", 1)
                vcons[sl] = vx2
                wge("sc", "ve", vx2)
                wge("sc", "te", v_sq_cons)
                vsq = emit("sc", lambda h, s=sl: h.activation(
                    sqb[:], rst[:, s, :], AF.Square), "sc", 1)
                wge("te", "sc", vsq)
                mm(2, PA[2][0:1, 0:512], ocol[:], rst[:, sl, 0:512], col=0)
                mm(2, PA[2][0:1, 512:1024], ocol[:], rst[:, sl, 512:1024], col=512)
                mm(3, PA[3][0:1, 0:512], ocol[:], sqb[:, 0:512], col=0)
                tcons[sl] = mm(3, PA[3][0:1, 512:1024], ocol[:], sqb[:, 512:1024],
                               col=512, sem="te", stop=(t == DT - 1))
                v_sq_cons = tcons[sl]
                wge("sp", "ve", vx2)
                dma("sp", x2sp[t], rst[:, sl, :])
            if upto == 6:
                wge("gp", "lsp", C["lsp"])
                emit("gp", lambda h: h.dma_start(
                    out=outT.rearrange("(t p) n -> p t n", p=128),
                    in_=x2sp.rearrange("t p n -> p t n")), "outp", 16)
                wge("sp", "outp", 16)
                return
            vb4 = ln_rows(C["te"])

            for t in range(DT):
                mm(3, PA[3][0:1, 16:24], ocol[:], rtw[:, t, :], col=0,
                   sem="te" if t == DT - 1 else None)
            wge("ve", "te", C["te"])
            v_crw = vop(lambda h: h.tensor_copy(crw[:], PA[3][0:1, 16:24]))
            release(3)

            v_n = vop(lambda h: h.tensor_scalar(row0[:], row0[:], -1.0, None, ALU.mult))
            wge("te", "ve", max(v_n, v_brow))
            mm(3, PA[3][:, 8:16], orow[:], brow[:], col=0, sem="te", stop=True)
            wge("ve", "te", C["te"])
            v_bb = vop(lambda h: h.tensor_copy(browb[:], PA[3][:, 8:16]))
            release(3)
            wge("te", "ve", v_n)
            for nb in range(8):
                mm(3, PA[3][:, nb:nb + 1], row1[0:1, nb * 128:(nb + 1) * 128],
                   o8[0:1, 0:1], col=0, sem="te" if nb == 7 else None)
            wge("ve", "te", C["te"])
            for nb in range(8):
                v_rcol = emit("ve", lambda h, b=nb: h.tensor_copy(
                    rcol[:, b:b + 1], PA[3][:, b:b + 1]),
                    "ve" if nb == 7 else None, 1)
            release(3)

            ncons = [0, 0]
            rcons = [0, 0]
            wge("te", "ve", v_rcol)
            for t in range(DT):
                sl = t % 2
                wge("sp", "ve", ncons[sl])
                wge("sp", "te", rcons[sl])
                vx = dma("sp", xst[:, sl, :], x2sp[t])
                wge("ve", "lsp", vx)
                wge("ve", "te", vb4)
                emit("ve", lambda h, s=sl: h.tensor_tensor(sqb[:], xst[:, s, :],
                                                           PA[0][:], ALU.mult))
                ncons[sl] = emit("ve", lambda h, t_=t: h.tensor_tensor(
                    xn1[:, t_, :], sqb[:], PA[1][:], ALU.subtract), "ve", 1)
                wge("te", "lsp", vx)
                for nb in range(8):
                    rcons[sl] = mm(2, PA[2][:, 64 + nb * 8: 72 + nb * 8],
                                   xst[:, sl, nb * 128:(nb + 1) * 128],
                                   rtw[:, t, :], col=0,
                                   sem="te" if nb == 7 else None)
            v_xn2 = C["ve"]
            release(0)
            release(1)
            wge("te", "ve", max(v_crw, v_brow, v_n))
            for nb in range(8):
                mm(2, PA[2][:, 64 + nb * 8: 72 + nb * 8],
                   row0[0:1, nb * 128:(nb + 1) * 128], crw[:], col=0,
                   sem="te" if nb == 7 else None)
            v_raw = C["te"]

            wge("ve", "te", v_raw)
            wge("ve", "ve", v_rcol)
            wge("ve", "ve", v_bb)
            for nb in range(8):
                reg = PA[2][:, 64 + nb * 8: 72 + nb * 8]
                vop(lambda h, P=reg, b=nb: h.scalar_tensor_tensor(
                    rsm[:, 0:8], P, rcol[:, b:b + 1], browb[:],
                    ALU.mult, ALU.add))
                vop(lambda h: h.tensor_reduce(rsm[:, 8:9], rsm[:, 0:8], AX.X, ALU.max))
                v_m = vop(lambda h: h.tensor_scalar(rsm[:, 9:10], rsm[:, 8:9], -1.0,
                                                    None, ALU.mult))
                wge("sc", "ve", v_m)
                vs = sop(lambda h: h.activation(rsm[:, 16:24], rsm[:, 0:8], AF.Exp,
                                                bias=rsm[:, 9:10],
                                                accum_out=rsm[:, 10:11]))
                wge("ve", "sc", vs)
                vop(lambda h: h.reciprocal(rsm[:, 11:12], rsm[:, 10:11]))
                vop(lambda h: h.tensor_scalar(rsm[:, 16:24], rsm[:, 16:24],
                                              rsm[:, 11:12], None, ALU.mult))
                vop(lambda h: h.tensor_reduce(rsm[:, 24:25], rsm[:, 16:24], AX.X,
                                              ALU.max))
                vop(lambda h: h.tensor_scalar(rsm[:, 32:40], rsm[:, 16:24],
                                              rsm[:, 24:25], 0.0, ALU.subtract,
                                              ALU.is_ge))
                vop(lambda h: h.tensor_scalar(rsm[:, 40:48], rsm[:, 32:40], -1e9,
                                              None, ALU.mult))
                vop(lambda h: h.tensor_tensor(rsm[:, 40:48], rsm[:, 16:24],
                                              rsm[:, 40:48], ALU.add))
                vop(lambda h: h.tensor_reduce(rsm[:, 25:26], rsm[:, 40:48], AX.X,
                                              ALU.max))
                vop(lambda h: h.tensor_scalar(rsm[:, 48:56], rsm[:, 40:48],
                                              rsm[:, 25:26], 0.0, ALU.subtract,
                                              ALU.is_ge))
                v_g2 = vop(lambda h: h.tensor_copy(rsm[:, 26:28], rsm[:, 24:26]))
                wge("sc", "ve", v_g2)
                vs = sop(lambda h: h.activation(rsm[:, 28:30], rsm[:, 26:28], AF.Exp,
                                                accum_out=rsm[:, 30:31]))
                wge("ve", "sc", vs)
                vop(lambda h: h.reciprocal(rsm[:, 31:32], rsm[:, 30:31]))
                vop(lambda h: h.tensor_scalar(rsm[:, 28:30], rsm[:, 28:30],
                                              rsm[:, 31:32], None, ALU.mult))
                vop(lambda h: h.tensor_scalar(rsm[:, 32:40], rsm[:, 32:40],
                                              rsm[:, 28:29], None, ALU.mult))
                vop(lambda h: h.tensor_scalar(rsm[:, 48:56], rsm[:, 48:56],
                                              rsm[:, 29:30], None, ALU.mult))
                v_w8 = vop(lambda h, b=nb: h.tensor_tensor(
                    wsb[:, b, :], rsm[:, 32:40], rsm[:, 48:56], ALU.add), sync=False)
            release(2)

            wge("te", "ve", v_w8)
            v_wt = 0
            for nb in range(8):
                wge("te", "ve", v_wt)
                mm(3, PA[3][0:8, 0:128], wsb[:, nb, :], idt[:], col=0,
                   sem="te", stop=True)
                wge("ve", "te", C["te"])
                v_wt = emit("ve", lambda h, b=nb: h.tensor_copy(
                    wTr[:, b * 128:(b + 1) * 128], PA[3][0:8, 0:128]), "ve", 1)
                release(3)
            wge("te", "ve", v_wt)
            mm(3, PA[3][0:1, 0:512], o8[:], wTr[:, 0:512], col=0)
            v_gs = mm(3, PA[3][0:1, 512:1024], o8[:], wTr[:, 512:1024], col=512,
                      sem="te", stop=True)
            wge("ve", "te", v_gs)
            v_gsr = vop(lambda h: h.tensor_copy(row1[:], PA[3][0:1, :]))
            release(3)

            wge("te", "ve", v_xn2)
            dcons = 0
            for lb in range(2):
                for eh in range(2):
                    wge("gp", "te", wrel[eh])
                    src = down_w[eh * 1024:(eh + 1) * 1024, lb * 512:(lb + 1) * 512]
                    dma("gp", wsl[eh], src.rearrange("(t p) d -> p t d", p=128))
                v_w = C["lgp"]
                wge("te", "lgp", v_w)
                wge("te", "sc", dcons)
                for ltr in range(4):
                    for eh in range(2):
                        for et in range(8):
                            for nh in range(2):
                                is_last = (eh == 1 and et == 7 and nh == 1)
                                wrel[eh] = mm(
                                    ltr, PA[ltr][:, nh * 512:(nh + 1) * 512],
                                    wsl[eh][:, et, ltr * 128:(ltr + 1) * 128],
                                    xn1[:, eh * 8 + et, nh * 512:(nh + 1) * 512],
                                    col=nh * 512, sem="te" if is_last else None)
                vte = C["te"]
                wge("sc", "te", vte)
                for ltr in range(4):
                    lt = lb * 4 + ltr
                    dcons = emit("sc", lambda h, p=ltr, l=lt: h.activation(
                        qsb[:, l, :], PA[p][:], AF.Gelu,
                        bias=colb[:, 16 + l:17 + l]), "sc" if ltr == 3 else None, 1)
                    release(ltr)
            v_gh = dcons

            v_te_done = C["te"]
            wge("sc", "te", v_te_done)
            for t in range(DT):
                v_gx = emit("sc", lambda h, t_=t: h.activation(
                    xn1[:, t_, :], xn1[:, t_, :], AF.Gelu),
                    "sc" if t == DT - 1 else None, 1)

            if upto == 7:
                wge("gp", "sc", v_gx)
                wge("gp", "ve", v_gsr)
                emit("gp", lambda h: h.dma_start(
                    out=outT[0:1024, :].rearrange("(t p) n -> p t n", p=128),
                    in_=qsb.ap()[:, 0:8, :]), "outp", 16)
                emit("gp", lambda h: h.dma_start(out=outT[1024:1032, :], in_=wTr[:]),
                     "outp", 16)
                emit("gp", lambda h: h.dma_start(out=outT[1032:1033, :], in_=row1[:]),
                     "outp", 16)
                emit("gp", lambda h: h.dma_start(out=outT[1040:1168, 0:8], in_=rcol[:]),
                     "outp", 16)
                wge("sp", "outp", 16 * 4)
                return

            HIDX = HID + [1024]
            moe_init = False
            v_gext = 0
            v_moe_ev = 0
            v_gr = 0
            for e in range(9):
                he = HIDX[e]
                groups = he // 1024
                shared = (e == 8)
                w1s = sh_w1 if shared else ew1[e]
                w2s = sh_w2 if shared else ew2[e]
                b1s = sh_b1 if shared else eb1[e]
                if not shared:
                    wge("gp", "te", C["te"])
                    wge("gp", "ve", v_gsr)
                    v_row = dma("gp", row0[0:1, :], wTr[e:e + 1, :])
                    wge("te", "ve", C["ve"])
                    wge("te", "lgp", v_row)
                    mm(2, PA[2][:, 0:512], orow[:], row0[0:1, 0:512], col=0)
                    v_bw = mm(2, PA[2][:, 512:1024], orow[:], row0[0:1, 512:1024],
                              col=512, sem="te", stop=True)
                    wge("ve", "te", v_bw)
                    v_bwc = vop(lambda h: h.tensor_copy(bwv[:], PA[2][:]))
                    release(2)
                wge("gp", "sc", C["sc"])
                vb1c = load_cols(b1cf[:, 0:groups * 8], b1s, groups * 8, pa=0)
                wge("sc", "ve", vb1c)
                v_gb1 = emit("sc", lambda h, g=groups: h.activation(
                    gb1c[:, 0:g * 8], b1cf[:, 0:g * 8], AF.Gelu), "sc", 1)
                for g in range(groups):
                    vws = [0, 0]
                    for lh in range(2):
                        wge("gp", "te", swrel[lh])
                        src = w1s[lh * 512:(lh + 1) * 512, g * 1024:(g + 1) * 1024]
                        vws[lh] = dma("gp", w1sl[lh],
                                      src.rearrange("(t p) d -> p t d", p=128))
                    vw2 = [0, 0]
                    for hh2 in range(2):
                        wge("gp", "te", kq_rel[hh2])
                        src = w2s[g * 1024 + hh2 * 512: g * 1024 + (hh2 + 1) * 512, :]
                        vw2[hh2] = dma("gp", w2sl[hh2],
                                       src.rearrange("(t p) d -> p t d", p=128))
                    if not shared:
                        wge("te", "lgp", vw2[1])
                        wge("te", "sc", v_gb1)
                        wge("te", "ve", v_gext)
                        for hh2 in range(2):
                            for htr in range(4):
                                htg = g * 8 + hh2 * 4 + htr
                                for nh in range(2):
                                    is_last = (hh2 == 1 and htr == 3 and nh == 1)
                                    mm(0, PA[0][0:1, nh * 512:(nh + 1) * 512],
                                       gb1c[:, htg:htg + 1],
                                       w2sl[hh2][:, htr, nh * 512:(nh + 1) * 512],
                                       col=nh * 512, sem="te" if is_last else None)
                        wge("ve", "te", C["te"])
                        wge("ve", "lgp", v_gr)
                        if g == 0:
                            v_gext = vop(lambda h: h.tensor_copy(row2[:], PA[0][0:1, :]),
                                         sync=False)
                        else:
                            v_gext = vop(lambda h: h.tensor_tensor(
                                row2[:], row2[:], PA[0][0:1, :], ALU.add), sync=False)
                        release(0)
                        wge("te", "ve", v_gext)
                    for nq in range(4):
                        no = nq * 256
                        for hh2 in range(2):
                            wge("te", "lgp", max(vws[1], vw2[1]))
                            wge("te", "sc", C["sc"])
                            for htr in range(4):
                                for lt8 in range(8):
                                    is_last = (htr == 3 and lt8 == 7)
                                    mm(hh2, PA[hh2][:, htr * 256:(htr + 1) * 256],
                                       w1sl[lt8 // 4][:, lt8 % 4,
                                                      (hh2 * 4 + htr) * 128:
                                                      (hh2 * 4 + htr + 1) * 128],
                                       qsb[:, lt8, no:no + 256],
                                       col=htr * 256, sem="te" if is_last else None)
                            vte = C["te"]
                            wge("sc", "te", vte)
                            wge("sc", "ve", v_moe_ev)
                            for htr in range(4):
                                htg = g * 8 + hh2 * 4 + htr
                                v_gel = emit("sc", lambda h, p=hh2, t_=htr, c2=htg:
                                             h.activation(
                                                 gtmp[:, p, t_ * 256:(t_ + 1) * 256],
                                                 PA[p][:, t_ * 256:(t_ + 1) * 256],
                                                 AF.Gelu, bias=b1cf[:, c2:c2 + 1]),
                                             "sc" if htr == 3 else None, 1)
                                release(hh2, htr * 256)
                            wge("ve", "sc", v_gel)
                            for htr in range(4):
                                sm = "ve" if htr == 3 else None
                                if shared:
                                    v_ga = emit("ve", lambda h, p=hh2, t_=htr:
                                                h.tensor_scalar(
                                                    ga[p][:, t_, :],
                                                    gtmp[:, p, t_ * 256:(t_ + 1) * 256],
                                                    0.1, None, ALU.mult), sm, 1)
                                else:
                                    v_ga = emit("ve", lambda h, p=hh2, t_=htr, n_=no:
                                                h.tensor_tensor(
                                                    ga[p][:, t_, :],
                                                    gtmp[:, p, t_ * 256:(t_ + 1) * 256],
                                                    bwv[:, n_:n_ + 256], ALU.mult),
                                                sm, 1)
                            wge("te", "ve", v_ga)
                        for lt8 in range(8):
                            pi = 2 + lt8 // 4
                            colo = (lt8 % 4) * 256
                            for hh2 in range(2):
                                for htr in range(4):
                                    is_last = (hh2 == 1 and htr == 3)
                                    mm(pi, PA[pi][:, colo:colo + 256],
                                       w2sl[hh2][:, htr, lt8 * 128:(lt8 + 1) * 128],
                                       ga[hh2][:, htr, :], col=colo,
                                       sem="te" if is_last else None)
                        vte = C["te"]
                        wge("ve", "te", vte)
                        for lt8 in range(8):
                            pi = 2 + lt8 // 4
                            colo = (lt8 % 4) * 256
                            sm = "ve" if lt8 == 7 else None
                            if not moe_init:
                                v_moe_ev = emit(
                                    "ve", lambda h, p=pi, c2=colo, l=lt8, n_=no:
                                    h.tensor_copy(moe[:, l, n_:n_ + 256],
                                                  PA[p][:, c2:c2 + 256]), sm, 1)
                            else:
                                v_moe_ev = emit(
                                    "ve", lambda h, p=pi, c2=colo, l=lt8, n_=no:
                                    h.scalar_tensor_tensor(
                                        moe[:, l, n_:n_ + 256], PA[p][:, c2:c2 + 256],
                                        1.0, moe[:, l, n_:n_ + 256],
                                        ALU.mult, ALU.add), sm, 1)
                            release(pi, colo)
                        wge("te", "ve", v_moe_ev)
                    moe_init = True
                    wrel[0] = wrel[1] = swrel[0] = swrel[1] = C["te"]
                if not shared:
                    v_ng = vop(lambda h: h.tensor_scalar(row2[:], row2[:], -1.0,
                                                         None, ALU.mult))
                    wge("gp", "ve", v_ng)
                    v_gr = dma("gp", gR[e:e + 1, :], row2[0:1, :])
                    v_gext = C["ve"]
            for lt8 in range(8):
                v_sb2 = vop(lambda h, l=lt8: h.tensor_scalar(
                    moe[:, l, :], moe[:, l, :], sb2c[:, l:l + 1], None, ALU.add),
                    sync=False)
            wge("ve", "ve", v_sb2)

            wge("te", "lgp", C["lgp"])
            wge("te", "ve", v_sb2)
            mm(0, PA[0][0:1, 0:512], o8[:], b2st[:, 0:512], col=0)
            mm(0, PA[0][0:1, 512:1024], o8[:], b2st[:, 512:1024], col=512)
            mm(0, PA[0][0:1, 0:512], n8[:], gR[:, 0:512], col=0)
            v_cr = mm(0, PA[0][0:1, 512:1024], n8[:], gR[:, 512:1024], col=512,
                      sem="te", stop=True)
            wge("ve", "te", v_cr)
            v_crow = vop(lambda h: h.tensor_copy(row0[:], PA[0][0:1, :]))
            release(0)
            wge("te", "ve", v_crow)
            for lt8 in range(8):
                for nq in range(4):
                    no = nq * 256
                    pi = 2 + lt8 // 4
                    colo = (lt8 % 4) * 256
                    mm(pi, PA[pi][:, colo:colo + 256],
                       row0[0:1, lt8 * 128:(lt8 + 1) * 128], row1[0:1, no:no + 256],
                       col=colo)
                    v_cm = mm(pi, PA[pi][:, colo:colo + 256],
                              gR[:, lt8 * 128:(lt8 + 1) * 128], wTr[:, no:no + 256],
                              col=colo, sem="te", stop=True)
                    wge("ve", "te", v_cm)
                    vc = emit("ve", lambda h, p=pi, c2=colo, l=lt8, n_=no:
                              h.scalar_tensor_tensor(
                                  moe[:, l, n_:n_ + 256], PA[p][:, c2:c2 + 256],
                                  1.0, moe[:, l, n_:n_ + 256], ALU.mult, ALU.add),
                              "ve", 1)
                    release(pi, colo)
                    wge("te", "ve", vc)
            v_moe_done = C["ve"]

            if upto == 8:
                wge("gp", "ve", v_moe_done)
                emit("gp", lambda h: h.dma_start(
                    out=outT[0:1024, :].rearrange("(t p) n -> p t n", p=128),
                    in_=moe.ap()), "outp", 16)
                wge("sp", "outp", 16)
                return

            wge("ve", "ve", v_moe_done)
            for lt8 in range(8):
                v_mbf = vop(lambda h, l=lt8: h.tensor_copy(moebf[:, l, :],
                                                           moe[:, l, :]), sync=False)
            wge("ve", "ve", v_mbf)
            wge("te", "ve", v_mbf)
            wge("te", "sc", v_gx)
            pcons = [0, 0]
            scons = [0, 0]
            fcons = [0, 0]
            up_rel = SV("s")
            co_rel = SV("b")
            for db in range(4):
                for lh in range(2):
                    wge("gp", "te", up_rel[lh])
                    src = up_w[lh * 512:(lh + 1) * 512, db * 512:(db + 1) * 512]
                    dma("gp", upsl[lh], src.rearrange("(t p) d -> p t d", p=128))
                for eh in range(2):
                    wge("gp", "te", co_rel[eh])
                    src = core_w[eh * 1024:(eh + 1) * 1024, db * 512:(db + 1) * 512]
                    dma("gp", cosl[eh], src.rearrange("(t p) d -> p t d", p=128))
                v_w = C["lgp"]
                wge("te", "lgp", v_w)
                for dtr in range(4):
                    dt = db * 4 + dtr
                    pi = dtr % 2
                    wge("te", "sc", pcons[pi])
                    for nh in range(2):
                        for lh in range(2):
                            for ltr in range(4):
                                up_rel[lh] = mm(
                                    pi, PA[pi][:, nh * 512:(nh + 1) * 512],
                                    upsl[lh][:, ltr, dtr * 128:(dtr + 1) * 128],
                                    moebf[:, lh * 4 + ltr, nh * 512:(nh + 1) * 512],
                                    col=nh * 512)
                        for eh in range(2):
                            for et in range(8):
                                is_last = (nh == 1 and eh == 1 and et == 7)
                                co_rel[eh] = mm(
                                    pi, PA[pi][:, nh * 512:(nh + 1) * 512],
                                    cosl[eh][:, et, dtr * 128:(dtr + 1) * 128],
                                    xn1[:, eh * 8 + et, nh * 512:(nh + 1) * 512],
                                    col=nh * 512, sem="te" if is_last else None)
                    vte = C["te"]
                    sl2 = dt % 2
                    wge("sc", "te", max(vte, v_sq_cons, scons[sl2]))
                    wge("sc", "lsp", fcons[sl2])
                    emit("sc", lambda h, p=pi, s=sl2, d=dt: h.activation(
                        xst[:, s, :], PA[p][:], AF.Identity,
                        bias=colb[:, 24 + d:25 + d]))
                    pcons[pi] = emit("sc", lambda h, s=sl2: h.activation(
                        sqb[:], xst[:, s, :], AF.Square), "sc", 1)
                    release(pi)
                    wge("te", "sc", pcons[pi])
                    mm(2, PA[2][0:1, 0:512], ocol[:], xst[:, sl2, 0:512], col=0)
                    mm(2, PA[2][0:1, 512:1024], ocol[:], xst[:, sl2, 512:1024], col=512)
                    mm(3, PA[3][0:1, 0:512], ocol[:], sqb[:, 0:512], col=0)
                    scons[sl2] = mm(3, PA[3][0:1, 512:1024], ocol[:], sqb[:, 512:1024],
                                    col=512, sem="te", stop=(dt == DT - 1))
                    v_sq_cons = scons[sl2]
                    wge("sp", "sc", pcons[pi])
                    fcons[sl2] = dma("sp", fsp[dt], xst[:, sl2, :])
            vb5 = ln_rows(C["te"])
            vcons = [0, 0]
            for t in range(DT):
                sl = t % 2
                wge("sp", "ve", vcons[sl])
                dma("sp", xst[:, sl, :], fsp[t])
                vx2 = dma("sp", rst[:, sl, :], x2sp[t])
                wge("ve", "lsp", vx2)
                wge("ve", "te", vb5)
                emit("ve", lambda h, s=sl: h.tensor_tensor(sqb[:], xst[:, s, :],
                                                           PA[0][:], ALU.mult))
                emit("ve", lambda h: h.tensor_tensor(sqb[:], sqb[:], PA[1][:],
                                                     ALU.subtract))
                vo = emit("ve", lambda h, s=sl: h.tensor_tensor(
                    rst[:, s, :], sqb[:], rst[:, s, :], ALU.add), "ve", 1)
                vcons[sl] = vo
                wge("sp", "ve", vo)
                emit("sp", lambda h, s=sl, t_=t: h.dma_start(
                    out=outT.rearrange("(a p) n -> a p n", p=128)[t_],
                    in_=rst[:, s, :]), "outp", 16)
            wge("sp", "outp", C["outp"])

        block.sync(lambda e: program("sp"))
        block.tensor(lambda e: program("te"))
        block.vector(lambda e: program("ve"))
        block.scalar(lambda e: program("sc"))
        block.gpsimd(lambda e: program("gp"))

    return nc


def prep(inputs):
    x = np.asarray(inputs["L0_attn_x"])
    xTfull = np.ascontiguousarray(x[0].T)
    base = {
        "wqT": np.ascontiguousarray(np.asarray(inputs["L0_attn_Wq"]).transpose(0, 2, 1)),
        "wkT": np.ascontiguousarray(np.asarray(inputs["L0_attn_Wk"]).transpose(0, 2, 1)),
        "wvT": np.ascontiguousarray(np.asarray(inputs["L0_attn_Wv"]).transpose(0, 2, 1)),
        "fcwT": np.ascontiguousarray(np.asarray(inputs["L0_attn_fc_w"]).T),
        "fc_b": np.asarray(inputs["L0_attn_fc_b"]),
        "down_w": np.asarray(inputs["L0_ffn_down_w"]),
        "down_b": np.asarray(inputs["L0_ffn_down_b"]),
        "up_w": np.asarray(inputs["L0_ffn_up_w"]),
        "up_b": np.asarray(inputs["L0_ffn_up_b"]),
        "router_w": np.asarray(inputs["L0_ffn_router_w"]),
        "router_b": np.asarray(inputs["L0_ffn_router_b"]),
        "sh_w1": np.asarray(inputs["L0_ffn_shared_w1"]),
        "sh_b1": np.asarray(inputs["L0_ffn_shared_b1"]),
        "sh_w2": np.asarray(inputs["L0_ffn_shared_w2"]),
        "sh_b2": np.asarray(inputs["L0_ffn_shared_b2"]),
        "core_w": np.asarray(inputs["L0_ffn_core_w"]),
        "core_b": np.asarray(inputs["L0_ffn_core_b"]),
    }
    for k in range(8):
        base[f"ew1_{k}"] = np.asarray(inputs[f"L0_ffn_e{k}_w1"])
        base[f"eb1_{k}"] = np.asarray(inputs[f"L0_ffn_e{k}_b1"])
        base[f"ew2_{k}"] = np.asarray(inputs[f"L0_ffn_e{k}_w2"])
        base[f"eb2_{k}"] = np.asarray(inputs[f"L0_ffn_e{k}_b2"])
    base = {k: np.ascontiguousarray(v, dtype=np.float32) for k, v in base.items()}
    in_maps = []
    for c in range(NC):
        m = dict(base)
        m["xT"] = np.ascontiguousarray(xTfull[:, c * NLOC:(c + 1) * NLOC])
        in_maps.append(m)
    return in_maps


def kernel(**inputs):
    in_maps = prep(inputs)
    nc = build()
    res = run_bass_kernel_spmd(nc, in_maps, core_ids=list(range(NC)))
    outTf = np.concatenate([r["outT"] for r in res.results], axis=1)
    return np.ascontiguousarray(outTf.T).reshape(1, 8192, 2048)


# version 02 (final)
# speedup vs baseline: 1.0236x; 1.0236x over previous
import math
from contextlib import ExitStack

import numpy as np

import concourse.bass as bass
import concourse.mybir as mybir
from concourse.bass_utils import run_bass_kernel_spmd

F32 = mybir.dt.float32
BF16 = mybir.dt.bfloat16
AF = mybir.ActivationFunctionType
ALU = mybir.AluOpType
AX = mybir.AxisListType

NC = 8
NLOC = 1024
D = 2048
DT = 16
L = 1024
H = 3
HID = [1024, 2048, 3072, 4096, 1024, 2048, 3072, 4096]
SCALE = 1.0 / math.sqrt(2048.0)
EXP_BIAS = -4.0
EPS = 1e-5


def _sel8_np():
    a = np.zeros((8, 1024), dtype=np.float32)
    for e in range(8):
        a[e, e * 128:(e + 1) * 128] = 1.0
    return a


def build(upto=9, tap=None):
    nc = bass.Bass("TRN2", target_bir_lowering=False)

    def din(name, shape):
        return nc.dram_tensor(name, list(shape), F32, kind="ExternalInput").ap()

    xT = din("xT", (D, NLOC))
    wqT = din("wqT", (H, D, D))
    wkT = din("wkT", (H, D, D))
    wvT = din("wvT", (H, D, D))
    fcwT = din("fcwT", (D, D))
    fc_b = din("fc_b", (D,))
    down_w = din("down_w", (D, L))
    down_b = din("down_b", (L,))
    up_w = din("up_w", (L, D))
    up_b = din("up_b", (D,))
    router_w = din("router_w", (D, 8))
    router_b = din("router_b", (8,))
    sh_w1 = din("sh_w1", (L, L))
    sh_b1 = din("sh_b1", (L,))
    sh_w2 = din("sh_w2", (L, L))
    sh_b2 = din("sh_b2", (L,))
    core_w = din("core_w", (D, D))
    core_b = din("core_b", (D,))
    ew1 = [din(f"ew1_{k}", (L, HID[k])) for k in range(8)]
    eb1 = [din(f"eb1_{k}", (HID[k],)) for k in range(8)]
    ew2 = [din(f"ew2_{k}", (HID[k], L)) for k in range(8)]
    eb2 = [din(f"eb2_{k}", (L,)) for k in range(8)]

    outT = nc.dram_tensor("outT", [D, NLOC], F32, kind="ExternalOutput").ap()

    k_in = nc.dram_tensor("k_in", [H * D, NLOC], BF16, kind="Internal").ap()
    k_g = nc.dram_tensor("k_g", [NC * H * D, NLOC], BF16, kind="Internal").ap()
    v_in = nc.dram_tensor("v_in", [H * 4 * NLOC, 512], BF16, kind="Internal").ap()
    v_g = nc.dram_tensor("v_g", [NC * H * 4 * NLOC, 512], BF16, kind="Internal").ap()
    s_in = nc.dram_tensor("s_in", [128, 192], F32, kind="Internal").ap()
    s_red = nc.dram_tensor("s_red", [128, 192], F32, kind="Internal").ap()
    es = nc.dram_tensor("es", [H * NC * NLOC, NLOC], BF16, kind="Internal").ap()
    fsp = nc.dram_tensor("fsp", [DT, 128, NLOC], F32, kind="Internal").ap()
    x2sp = nc.dram_tensor("x2sp", [DT, 128, NLOC], F32, kind="Internal").ap()

    ident = nc.inline_tensor(np.eye(128, dtype=np.float32), "c_ident").ap()
    sel8 = nc.inline_tensor(_sel8_np(), "c_sel8").ap()
    c_onescol = nc.inline_tensor(np.ones((128, 1), dtype=np.float32), "c_onescol").ap()
    c_onesrow = nc.inline_tensor(np.ones((1, 128), dtype=np.float32), "c_onesrow").ap()
    c_ones8 = nc.inline_tensor(np.ones((8, 1), dtype=np.float32), "c_ones8").ap()
    c_neg8 = nc.inline_tensor(np.full((8, 1), -1.0, dtype=np.float32), "c_neg8").ap()
    cst_np = np.zeros((128, 4), dtype=np.float32)
    cst_np[:, 0] = EXP_BIAS
    cst_np[:, 1] = EPS
    cst_np[:, 2] = 1.0
    c_cst = nc.inline_tensor(cst_np, "c_cst").ap()
    lam = (np.float32(1e-7) * np.array([2048.0 * h for h in HID], dtype=np.float32))
    c_lam = nc.inline_tensor(lam.reshape(1, 8).astype(np.float32), "c_lam").ap()

    with ExitStack() as ctx:
        E = ctx.enter_context

        xn1 = E(nc.sbuf_tensor("b_xn1", [128, DT, NLOC], BF16))
        qsb = E(nc.sbuf_tensor("b_q", [128, DT, NLOC], BF16))
        bw0 = E(nc.sbuf_tensor("b_bw0", [128, 4096], BF16))
        bw1 = E(nc.sbuf_tensor("b_bw1", [128, 4096], BF16))
        sw0 = E(nc.sbuf_tensor("b_sw0", [128, 4096], BF16))
        sw1 = E(nc.sbuf_tensor("b_sw1", [128, 4096], BF16))
        esb = E(nc.sbuf_tensor("b_es", [128, 8192], BF16))
        moe = E(nc.sbuf_tensor("b_moe", [128, 8, NLOC], F32))
        xst = E(nc.sbuf_tensor("b_xst", [128, 2, NLOC], F32))
        rst = E(nc.sbuf_tensor("b_rst", [128, 2, NLOC], F32))
        sqb = E(nc.sbuf_tensor("b_sq", [128, NLOC], F32))
        gtmp = E(nc.sbuf_tensor("b_gtmp", [128, 2, 1024], BF16))
        ga = [E(nc.sbuf_tensor(f"b_ga{i}", [128, 4, 256], BF16)) for i in range(2)]
        bwv = E(nc.sbuf_tensor("b_bwv", [128, 1024], BF16))
        row0 = E(nc.sbuf_tensor("b_row0", [1, 1024], F32))
        row1 = E(nc.sbuf_tensor("b_row1", [1, 1024], F32))
        row2 = E(nc.sbuf_tensor("b_row2", [1, 1024], F32))
        wTr = E(nc.sbuf_tensor("b_wT", [8, 1024], F32))
        gR = E(nc.sbuf_tensor("b_gR", [8, 1024], F32))
        b2st = E(nc.sbuf_tensor("b_b2st", [8, 1024], F32))
        ssb = E(nc.sbuf_tensor("b_s", [128, 192], F32))
        isb = E(nc.sbuf_tensor("b_is", [128, 192], F32))
        colb = E(nc.sbuf_tensor("b_colb", [128, 56], F32))
        gb1c = E(nc.sbuf_tensor("b_gb1", [128, 32], BF16))
        b1cf = E(nc.sbuf_tensor("b_b1c", [128, 32], F32))
        rsm = E(nc.sbuf_tensor("b_rsm", [128, 64], F32))
        rcol = E(nc.sbuf_tensor("b_rcol", [128, 8], F32))
        wsb = E(nc.sbuf_tensor("b_wsb", [128, 8, 8], F32))
        rtw = E(nc.sbuf_tensor("b_rtw", [128, DT, 8], F32))
        idt = E(nc.sbuf_tensor("b_idt", [128, 128], F32))
        ocol = E(nc.sbuf_tensor("b_ocol", [128, 1], F32))
        orow = E(nc.sbuf_tensor("b_orow", [1, 128], F32))
        o8 = E(nc.sbuf_tensor("b_o8", [8, 1], F32))
        n8 = E(nc.sbuf_tensor("b_n8", [8, 1], F32))
        cstb = E(nc.sbuf_tensor("b_cstb", [128, 4], F32))
        lamb = E(nc.sbuf_tensor("b_lamb", [1, 8], F32))
        brow = E(nc.sbuf_tensor("b_brow", [1, 8], F32))
        crw = E(nc.sbuf_tensor("b_crw", [1, 8], F32))
        sb2c = E(nc.sbuf_tensor("b_sb2c", [128, 8], F32))
        browb = E(nc.sbuf_tensor("b_browb", [128, 8], F32))
        bstg = E(nc.sbuf_tensor("b_bstg", [32, 128], F32))

        PA = [E(nc.psum_tensor(f"p_a{i}", [128, NLOC], F32)) for i in range(4)]

        SEM = {}
        for s in ("lsp", "lgp", "te", "ve", "sc", "ck", "cs", "cv", "outp"):
            SEM[s] = E(nc.semaphore("sem_" + s))

        block = E(nc.Block())

        wsl = [bw0.ap().rearrange("p (t d) -> p t d", d=512),
               bw1.ap().rearrange("p (t d) -> p t d", d=512)]
        es_k = esb.ap().rearrange("p (t m) -> p t m", m=1024)
        es_v = esb.ap().rearrange("p (q t m) -> p q t m", q=2, m=512)
        sw_v = [sw0.ap().rearrange("p (t m) -> p t m", m=512),
                sw1.ap().rearrange("p (t m) -> p t m", m=512)]
        es_st = esb.ap().rearrange("p (s t m) -> p s t m", s=2, m=1024)
        kpc = [wsl[0][:, 0:4, :], wsl[1][:, 0:4, :]]
        ppc = [sw0.ap().rearrange("p (t m) -> p t m", m=1024),
               sw1.ap().rearrange("p (t m) -> p t m", m=1024)]
        w1sl = ppc
        w2sl = [bw0.ap().rearrange("p (t m) -> p t m", m=1024),
                bw1.ap().rearrange("p (t m) -> p t m", m=1024)]
        upsl = [sw_v[0][:, 0:4, :], sw_v[1][:, 0:4, :]]
        cosl = wsl
        moebf = esb.ap().rearrange("p (t m) -> p t m", m=1024)
        xr = xT.rearrange("(t p) n -> t p n", p=128)

        def program(who):
            EN = {"sp": nc.sync, "te": nc.tensor, "ve": nc.vector,
                  "sc": nc.scalar, "gp": nc.gpsimd}
            C = {s: 0 for s in SEM}
            fresh = {}
            rel = {"b0": 0, "b1": 0, "s0": 0, "s1": 0}

            class SV:
                def __init__(self, pfx):
                    self.p = pfx

                def __getitem__(self, i):
                    return rel[self.p + str(i)]

                def __setitem__(self, i, v):
                    rel[self.p + str(i)] = v

            class PV:
                def __getitem__(self, i):
                    return max(rel["b" + str(i)], rel["s" + str(i)])

                def __setitem__(self, i, v):
                    rel["b" + str(i)] = v
                    rel["s" + str(i)] = v

            evc = [0]

            def emit(e, builder, sem=None, n=1):
                ins = builder(EN[e]) if who == e else None
                if sem is not None:
                    C[sem] += n
                    if ins is not None:
                        ins.then_inc(SEM[sem], n)
                return C[sem] if sem is not None else None

            def wge(e, sem, val=None):
                v = C[sem] if val is None else val
                if v > 0 and who == e:
                    EN[e].wait_ge(SEM[sem], v)

            def dma(e, out, in_, **kw):
                q = "lsp" if e == "sp" else "lgp"
                return emit(e, lambda h: h.dma_start(out=out, in_=in_, **kw), q, 16)

            def mm(pi, out_ap, lhsT, rhs, col=0, sem=None, stop=False):
                key = (pi, col // 512)
                st = fresh.get(key, True)
                fresh[key] = False
                return emit("te", lambda h: h.matmul(out_ap, lhsT, rhs, start=st,
                                                     stop=stop, skip_group_check=True),
                            sem, 1)

            def release(pi, col=None):
                if col is None:
                    fresh[(pi, 0)] = True
                    fresh[(pi, 1)] = True
                else:
                    fresh[(pi, col // 512)] = True

            def vop(builder, sync=True):
                v = emit("ve", builder, "ve", 1)
                if sync:
                    wge("ve", "ve", v)
                return v

            def sop(builder, sync=True):
                v = emit("sc", builder, "sc", 1)
                if sync:
                    wge("sc", "sc", v)
                return v

            def load_cols(dst, vec, nt, pa=3):
                wge("gp", "te", C["te"])
                vstg = dma("gp", bstg[0:nt, :], vec.rearrange("(t p) -> t p", p=128))
                wge("te", "lgp", vstg)
                v = mm(pa, PA[pa][:, 0:nt], bstg[0:nt, :], idt[0:nt, 0:nt],
                       col=0, sem="te", stop=True)
                wge("ve", "te", v)
                vc = vop(lambda h: h.tensor_copy(dst, PA[pa][:, 0:nt]), sync=False)
                release(pa)
                return vc

            dma("sp", idt[:], ident)
            dma("sp", ocol[:], c_onescol)
            dma("sp", orow[:], c_onesrow)
            dma("sp", o8[:], c_ones8)
            dma("sp", n8[:], c_neg8)
            dma("sp", cstb[:], c_cst)
            dma("sp", lamb[:], c_lam)
            dma("sp", brow[:], router_b[None, :])
            dma("sp", rtw.ap(), router_w.rearrange("(t p) e -> p t e", p=128))
            for k in range(8):
                dma("sp", b2st[k:k + 1, :], eb2[k][None, :])
            v_const = C["lsp"]
            wge("te", "lsp", v_const)
            load_cols(colb[:, 0:16], fc_b, 16)
            load_cols(colb[:, 16:24], down_b, 8)
            load_cols(colb[:, 24:40], up_b, 16)
            load_cols(colb[:, 40:56], core_b, 16)
            load_cols(sb2c[:], sh_b2, 8)
            v_cols = C["ve"]
            wge("ve", "lsp", v_const)
            wge("ve", "ve", v_cols)
            vop(lambda h: h.tensor_tensor(colb[:, 24:40], colb[:, 24:40],
                                          colb[:, 40:56], ALU.add))
            vop(lambda h: h.tensor_scalar(sb2c[:], sb2c[:], 0.1, None, ALU.mult))
            v_brow = vop(lambda h: h.tensor_tensor(brow[:], brow[:], lamb[:],
                                                   ALU.subtract))

            cons = [0, 0]
            v_sq_cons = 0
            for t in range(DT):
                sl = t % 2
                wge("sp", "te", cons[sl])
                vx = dma("sp", xst[:, sl, :], xr[t])
                wge("sc", "lsp", vx)
                wge("sc", "te", v_sq_cons)
                vs = emit("sc", lambda h, s=sl: h.activation(sqb[:], xst[:, s, :],
                                                             AF.Square), "sc", 1)
                wge("te", "lsp", vx)
                mm(2, PA[2][0:1, 0:512], ocol[:], xst[:, sl, 0:512], col=0)
                mm(2, PA[2][0:1, 512:1024], ocol[:], xst[:, sl, 512:1024], col=512)
                wge("te", "sc", vs)
                mm(3, PA[3][0:1, 0:512], ocol[:], sqb[:, 0:512], col=0)
                cons[sl] = mm(3, PA[3][0:1, 512:1024], ocol[:], sqb[:, 512:1024],
                              col=512, sem="te", stop=(t == DT - 1))
                v_sq_cons = cons[sl]
            v_stats = C["te"]

            def ln_rows(v_te_done):
                wge("ve", "te", v_te_done)
                vop(lambda h: h.tensor_copy(row0[:], PA[2][0:1, :]))
                vop(lambda h: h.tensor_copy(row1[:], PA[3][0:1, :]))
                release(2)
                release(3)
                vop(lambda h: h.tensor_scalar(row0[:], row0[:], 1.0 / 2048.0, None,
                                              ALU.mult))
                vop(lambda h: h.tensor_scalar(row1[:], row1[:], 1.0 / 2048.0, None,
                                              ALU.mult))
                vop(lambda h: h.tensor_tensor(row2[:], row0[:], row0[:], ALU.mult))
                vv = vop(lambda h: h.tensor_tensor(row1[:], row1[:], row2[:],
                                                   ALU.subtract))
                wge("sc", "ve", vv)
                vs = sop(lambda h: h.activation(row1[:], row1[:], AF.Sqrt,
                                                bias=cstb[0:1, 1:2]))
                wge("ve", "sc", vs)
                vop(lambda h: h.reciprocal(row1[:], row1[:]))
                vr = vop(lambda h: h.tensor_tensor(row2[:], row0[:], row1[:], ALU.mult))
                wge("te", "ve", vr)
                wge("te", "sc", C["sc"])
                release(0)
                release(1)
                mm(0, PA[0][:, 0:512], orow[:], row1[0:1, 0:512], col=0)
                mm(0, PA[0][:, 512:1024], orow[:], row1[0:1, 512:1024], col=512)
                mm(1, PA[1][:, 0:512], orow[:], row2[0:1, 0:512], col=0)
                vb = mm(1, PA[1][:, 512:1024], orow[:], row2[0:1, 512:1024],
                        col=512, sem="te", stop=True)
                return vb

            vb1 = ln_rows(v_stats)
            cons = [0, 0]
            for t in range(DT):
                sl = t % 2
                wge("sp", "ve", cons[sl])
                vx = dma("sp", xst[:, sl, :], xr[t])
                wge("ve", "lsp", vx)
                wge("ve", "te", vb1)
                emit("ve", lambda h, s=sl: h.tensor_tensor(sqb[:], xst[:, s, :],
                                                           PA[0][:], ALU.mult))
                cons[sl] = emit("ve", lambda h, t_=t: h.tensor_tensor(
                    xn1[:, t_, :], sqb[:], PA[1][:], ALU.subtract), "ve", 1)
            v_xn1 = C["ve"]
            release(0)
            release(1)

            if upto == 1:
                wge("gp", "ve", v_xn1)
                emit("gp", lambda h: h.dma_start(
                    out=outT.rearrange("(t p) n -> p t n", p=128), in_=xn1.ap()),
                    "outp", 16)
                wge("sp", "outp", 16)
                return

            def wpieces(wap):
                out = []
                for db in range(4):
                    for eh in range(2):
                        src = wap[eh * 1024:(eh + 1) * 1024, db * 512:(db + 1) * 512]
                        out.append(src.rearrange("(t p) d -> p t d", p=128))
                return out

            wge("te", "ve", v_xn1)
            wrel = SV("b")
            swrel = SV("s")
            st_rel = 0

            def proj_K(hh):
                nonlocal st_rel
                pieces = wpieces(wkT[hh])
                vev = evc[0]
                for dh in range(2):
                    for db in (2 * dh, 2 * dh + 1):
                        for eh in range(2):
                            k = db * 2 + eh
                            sl = k % 2
                            wge("gp", "te", wrel[sl])
                            vw = dma("gp", wsl[sl], pieces[k])
                            wge("te", "lgp", vw)
                            if eh == 0:
                                wge("te", "ve", vev)
                            for dtr in range(4):
                                for et in range(8):
                                    for nh in range(2):
                                        is_last = (dtr == 3 and et == 7 and nh == 1)
                                        wrel[sl] = mm(
                                            dtr, PA[dtr][:, nh * 512:(nh + 1) * 512],
                                            wsl[sl][:, et, dtr * 128:(dtr + 1) * 128],
                                            xn1[:, eh * 8 + et, nh * 512:(nh + 1) * 512],
                                            col=nh * 512,
                                            sem="te" if is_last else None)
                        vte = C["te"]
                        wge("ve", "te", vte)
                        wge("ve", "lsp", st_rel)
                        for dtr in range(4):
                            vev = emit("ve", lambda h, d=dtr, b=db: h.tensor_copy(
                                es_k[:, (b % 2) * 4 + d, :], PA[d][:]),
                                "ve" if dtr == 3 else None, 1)
                            release(dtr)
                        evc[0] = vev
                    wge("sp", "ve", vev)
                    dst = k_in[hh * D + dh * 1024: hh * D + (dh + 1) * 1024, :]
                    st_rel = dma("sp", dst.rearrange("(t p) m -> p t m", p=128), es_k)

            for hh in range(H):
                proj_K(hh)

            def proj_V(hh):
                nonlocal st_rel
                pieces = wpieces(wvT[hh])
                vev = evc[0]
                for db in range(4):
                    for eh in range(2):
                        k = db * 2 + eh
                        sl = k % 2
                        wge("gp", "te", wrel[sl])
                        vw = dma("gp", wsl[sl], pieces[k])
                        wge("te", "lgp", vw)
                        if eh == 0:
                            wge("te", "ve", vev)
                        for mt in range(8):
                            pi = mt // 2
                            colo = (mt % 2) * 512
                            for et in range(8):
                                is_last = (mt == 7 and et == 7)
                                wrel[sl] = mm(
                                    pi, PA[pi][:, colo:colo + 512],
                                    xn1[:, eh * 8 + et, mt * 128:(mt + 1) * 128],
                                    wsl[sl][:, et, :], col=colo,
                                    sem="te" if is_last else None)
                    vte = C["te"]
                    wge("ve", "te", vte)
                    if db == 0:
                        wge("ve", "lsp", st_rel)
                    for mt in range(8):
                        pi = mt // 2
                        colo = (mt % 2) * 512
                        sm = "ve" if mt == 7 else None
                        if db < 2:
                            vev = emit("ve", lambda h, p=pi, c=colo, qq=db, m=mt:
                                       h.tensor_copy(es_v[:, qq, m, :],
                                                     PA[p][:, c:c + 512]), sm, 1)
                        else:
                            vev = emit("ve", lambda h, p=pi, c=colo, qq=db, m=mt:
                                       h.tensor_copy(sw_v[qq - 2][:, m, :],
                                                     PA[p][:, c:c + 512]), sm, 1)
                        release(pi, colo)
                    evc[0] = vev
                wge("sp", "ve", vev)
                for q in range(4):
                    dst = v_in[(hh * 4 + q) * NLOC:(hh * 4 + q + 1) * NLOC, :]
                    src = es_v[:, q, :, :] if q < 2 else sw_v[q - 2]
                    st_rel = dma("sp", dst.rearrange("(t p) m -> p t m", p=128), src)

            for hh in range(H):
                proj_V(hh)
            wge("gp", "lsp", C["lsp"])
            wge("gp", "lgp", C["lgp"])
            emit("gp", lambda h: h.collective_compute(
                "AllGather", ALU.bypass, replica_groups=[list(range(NC))],
                ins=[k_in], outs=[k_g]), "ck", 1)
            wge("gp", "ck", 1)
            emit("gp", lambda h: h.collective_compute(
                "AllGather", ALU.bypass, replica_groups=[list(range(NC))],
                ins=[v_in], outs=[v_g]), "cv", 1)
            wge("gp", "cv", 1)

            if upto == 2:
                wge("gp", "lsp", st_rel)
                if tap == "k":
                    src = k_in[0:D, :]
                else:
                    src = v_in.rearrange("(a b) m -> a (b m)", b=2)[0:D, 0:NLOC]
                emit("gp", lambda h: h.dma_start(
                    out=outT.rearrange("(t p) n -> p t n", p=128),
                    in_=src.rearrange("(t p) n -> p t n", p=128)), "outp", 16)
                wge("sp", "outp", 16)
                return

            wge("sp", "ck", 1)
            wge("gp", "cv", 1)
            kq_rel = SV("b")
            st_store = [0, 0]
            v_exp = 0
            for hh in range(H):
                pieces = wpieces(wqT[hh])
                qrel = evc[0]
                wge("te", "sc", v_exp)
                for db in range(4):
                    for eh in range(2):
                        k = db * 2 + eh
                        sl = k % 2
                        wge("gp", "te", wrel[sl])
                        vw = dma("gp", wsl[sl], pieces[k])
                        wge("te", "lgp", vw)
                        if eh == 0:
                            wge("te", "ve", qrel)
                        for dtr in range(4):
                            for et in range(8):
                                for nh in range(2):
                                    is_last = (dtr == 3 and et == 7 and nh == 1)
                                    wrel[sl] = mm(
                                        dtr, PA[dtr][:, nh * 512:(nh + 1) * 512],
                                        wsl[sl][:, et, dtr * 128:(dtr + 1) * 128],
                                        xn1[:, eh * 8 + et, nh * 512:(nh + 1) * 512],
                                        col=nh * 512,
                                        sem="te" if is_last else None)
                    vte = C["te"]
                    wge("ve", "te", vte)
                    for dtr in range(4):
                        qrel = emit("ve", lambda h, d=dtr, b=db: h.tensor_copy(
                            qsb[:, b * 4 + d, :], PA[d][:]),
                            "ve" if dtr == 3 else None, 1)
                        release(dtr)
                    evc[0] = qrel
                wge("te", "ve", qrel)

                for r in range(NC):
                    for mh in range(2):
                        chunk = (hh * NC + r) * 2 + mh
                        base = r * (H * D) + hh * D
                        for dq in range(4):
                            sl2 = dq % 2
                            wge("sp", "te", kq_rel[sl2])
                            src = k_g[base + dq * 512: base + (dq + 1) * 512,
                                      mh * 512:(mh + 1) * 512]
                            vw = dma("sp", kpc[sl2],
                                     src.rearrange("(t p) m -> p t m", p=128))
                            wge("te", "lsp", vw)
                            if dq == 0:
                                wge("te", "sc", v_exp)
                            for mtr in range(4):
                                for dtr in range(4):
                                    dtg = dq * 4 + dtr
                                    for nh in range(2):
                                        is_last = (mtr == 3 and dtr == 3 and nh == 1)
                                        kq_rel[sl2] = mm(
                                            mtr, PA[mtr][:, nh * 512:(nh + 1) * 512],
                                            kpc[sl2][:, dtr, mtr * 128:(mtr + 1) * 128],
                                            qsb[:, dtg, nh * 512:(nh + 1) * 512],
                                            col=nh * 512,
                                            sem="te" if is_last else None)
                        vte = C["te"]
                        ssl = chunk % 2
                        wge("sc", "te", vte)
                        wge("sc", "lsp", st_store[ssl])
                        for mtr in range(4):
                            coli = hh * 64 + r * 8 + mh * 4 + mtr
                            v_exp = emit("sc", lambda h, m=mtr, c2=coli, s=ssl:
                                         h.activation(es_st[:, s, m, :], PA[m][:],
                                                      AF.Exp, bias=cstb[:, 0:1],
                                                      scale=SCALE,
                                                      accum_out=ssb[:, c2:c2 + 1]),
                                         "sc" if mtr == 3 else None, 1)
                            release(mtr)
                        wge("sp", "sc", v_exp)
                        ro = hh * NC * NLOC + r * NLOC + mh * 512
                        st_store[ssl] = dma(
                            "sp", es[ro:ro + 512, :].rearrange("(t p) n -> p t n", p=128),
                            es_st[:, ssl, :, :])
            v_s_done = C["sc"]

            wge("gp", "sc", v_s_done)
            wge("gp", "lsp", C["lsp"])
            v1 = dma("gp", s_in, ssb[:])
            wge("gp", "lgp", v1)
            emit("gp", lambda h: h.collective_compute(
                "AllReduce", ALU.add, replica_groups=[list(range(NC))],
                ins=[s_in], outs=[s_red]), "cs", 1)
            wge("gp", "cs", 1)
            v2 = dma("gp", ssb[:], s_red)
            wge("ve", "lgp", v2)
            vop(lambda h: h.tensor_scalar(isb[:], ssb[:], 3.0, None, ALU.mult))
            v_inv = vop(lambda h: h.reciprocal(isb[:], isb[:]))

            if upto == 3:
                wge("gp", "ve", v_inv)
                if tap == "es":
                    emit("gp", lambda h: h.dma_start(
                        out=outT.rearrange("(t p) n -> p t n", p=128),
                        in_=es[0:2048, :].rearrange("(t p) n -> p t n", p=128)),
                        "outp", 16)
                else:
                    emit("gp", lambda h: h.dma_start(out=outT[0:128, 0:192],
                                                     in_=ssb[:]), "outp", 16)
                wge("sp", "outp", 16)
                return

            wge("sp", "cv", 1)
            wge("te", "sc", v_exp)
            p_rel = PV()
            for q in range(4):
                idx = 0
                for r in range(NC):
                    for hh in range(H):
                        for mh in range(2):
                            sl = idx % 2
                            idx += 1
                            wge("sp", "te", p_rel[sl])
                            ro = hh * NC * NLOC + r * NLOC + mh * 512
                            vp = dma("sp", ppc[sl],
                                     es[ro:ro + 512, :].rearrange(
                                         "(t p) n -> p t n", p=128))
                            vo2 = r * (H * 4 * NLOC) + (hh * 4 + q) * NLOC + mh * 512
                            vv = dma("sp", kpc[sl],
                                     v_g[vo2:vo2 + 512, :].rearrange(
                                         "(t p) m -> p t m", p=128))
                            wge("ve", "lsp", vv)
                            for mtr in range(4):
                                ci = hh * 64 + r * 8 + mh * 4 + mtr
                                vsc2 = emit("ve", lambda h, m=mtr, c2=ci, s=sl:
                                            h.tensor_scalar(kpc[s][:, m, :],
                                                            kpc[s][:, m, :],
                                                            isb[:, c2:c2 + 1],
                                                            None, ALU.mult),
                                            "ve" if mtr == 3 else None, 1)
                            wge("te", "lsp", vp)
                            wge("te", "ve", vsc2)
                            for mtr in range(4):
                                for dtr in range(4):
                                    for nh in range(2):
                                        is_last = (mtr == 3 and dtr == 3 and nh == 1)
                                        p_rel[sl] = mm(
                                            dtr, PA[dtr][:, nh * 512:(nh + 1) * 512],
                                            kpc[sl][:, mtr, dtr * 128:(dtr + 1) * 128],
                                            ppc[sl][:, mtr, nh * 512:(nh + 1) * 512],
                                            col=nh * 512,
                                            sem="te" if is_last else None)
                vte = C["te"]
                wge("ve", "te", vte)
                for dtr in range(4):
                    vev = emit("ve", lambda h, d=dtr, qq=q: h.tensor_copy(
                        qsb[:, qq * 4 + d, :], PA[d][:]),
                        "ve" if dtr == 3 else None, 1)
                    release(dtr)
                wge("te", "ve", vev)
            v_attn = C["ve"]

            if upto == 5:
                wge("gp", "ve", v_attn)
                emit("gp", lambda h: h.dma_start(
                    out=outT.rearrange("(t p) n -> p t n", p=128), in_=qsb.ap()),
                    "outp", 16)
                wge("sp", "outp", 16)
                return

            wge("te", "ve", v_attn)
            pcons = [0, 0]
            scons = [0, 0]
            fcons = [0, 0]
            for jb in range(4):
                for dh in range(2):
                    wge("gp", "te", wrel[dh])
                    src = fcwT[dh * 1024:(dh + 1) * 1024, jb * 512:(jb + 1) * 512]
                    dma("gp", wsl[dh], src.rearrange("(t p) d -> p t d", p=128))
                v_w = C["lgp"]
                wge("te", "lgp", v_w)
                for jtr in range(4):
                    jt = jb * 4 + jtr
                    pi = jtr % 2
                    wge("te", "sc", pcons[pi])
                    for dh in range(2):
                        for dt8 in range(8):
                            for nh in range(2):
                                is_last = (dh == 1 and dt8 == 7 and nh == 1)
                                wrel[dh] = mm(
                                    pi, PA[pi][:, nh * 512:(nh + 1) * 512],
                                    wsl[dh][:, dt8, jtr * 128:(jtr + 1) * 128],
                                    qsb[:, dh * 8 + dt8, nh * 512:(nh + 1) * 512],
                                    col=nh * 512, sem="te" if is_last else None)
                    vte = C["te"]
                    sl2 = jt % 2
                    wge("sc", "te", max(vte, v_sq_cons, scons[sl2]))
                    wge("sc", "lsp", fcons[sl2])
                    emit("sc", lambda h, p=pi, s=sl2, j=jt: h.activation(
                        xst[:, s, :], PA[p][:], AF.Identity, bias=colb[:, j:j + 1]))
                    pcons[pi] = emit("sc", lambda h, s=sl2: h.activation(
                        sqb[:], xst[:, s, :], AF.Square), "sc", 1)
                    release(pi)
                    wge("te", "sc", pcons[pi])
                    mm(2, PA[2][0:1, 0:512], ocol[:], xst[:, sl2, 0:512], col=0)
                    mm(2, PA[2][0:1, 512:1024], ocol[:], xst[:, sl2, 512:1024], col=512)
                    mm(3, PA[3][0:1, 0:512], ocol[:], sqb[:, 0:512], col=0)
                    scons[sl2] = mm(3, PA[3][0:1, 512:1024], ocol[:], sqb[:, 512:1024],
                                    col=512, sem="te", stop=(jt == DT - 1))
                    v_sq_cons = scons[sl2]
                    wge("sp", "sc", pcons[pi])
                    fcons[sl2] = dma("sp", fsp[jt], xst[:, sl2, :])
            vb3 = ln_rows(C["te"])
            vcons = [0, 0]
            tcons = [0, 0]
            for t in range(DT):
                sl = t % 2
                wge("sp", "ve", vcons[sl])
                wge("sp", "te", tcons[sl])
                dma("sp", xst[:, sl, :], fsp[t])
                vxr = dma("sp", rst[:, sl, :], xr[t])
                wge("ve", "lsp", vxr)
                wge("ve", "te", max(vb3, v_sq_cons))
                emit("ve", lambda h, s=sl: h.tensor_tensor(sqb[:], xst[:, s, :],
                                                           PA[0][:], ALU.mult))
                emit("ve", lambda h: h.tensor_tensor(sqb[:], sqb[:], PA[1][:],
                                                     ALU.subtract))
                vx2 = emit("ve", lambda h, s=sl: h.tensor_tensor(
                    rst[:, s, :], sqb[:], rst[:, s, :], ALU.add), "ve", 1)
                vcons[sl] = vx2
                wge("sc", "ve", vx2)
                wge("sc", "te", v_sq_cons)
                vsq = emit("sc", lambda h, s=sl: h.activation(
                    sqb[:], rst[:, s, :], AF.Square), "sc", 1)
                wge("te", "sc", vsq)
                mm(2, PA[2][0:1, 0:512], ocol[:], rst[:, sl, 0:512], col=0)
                mm(2, PA[2][0:1, 512:1024], ocol[:], rst[:, sl, 512:1024], col=512)
                mm(3, PA[3][0:1, 0:512], ocol[:], sqb[:, 0:512], col=0)
                tcons[sl] = mm(3, PA[3][0:1, 512:1024], ocol[:], sqb[:, 512:1024],
                               col=512, sem="te", stop=(t == DT - 1))
                v_sq_cons = tcons[sl]
                wge("sp", "ve", vx2)
                dma("sp", x2sp[t], rst[:, sl, :])
            if upto == 6:
                wge("gp", "lsp", C["lsp"])
                emit("gp", lambda h: h.dma_start(
                    out=outT.rearrange("(t p) n -> p t n", p=128),
                    in_=x2sp.rearrange("t p n -> p t n")), "outp", 16)
                wge("sp", "outp", 16)
                return
            vb4 = ln_rows(C["te"])

            for t in range(DT):
                mm(3, PA[3][0:1, 16:24], ocol[:], rtw[:, t, :], col=0,
                   sem="te" if t == DT - 1 else None)
            wge("ve", "te", C["te"])
            v_crw = vop(lambda h: h.tensor_copy(crw[:], PA[3][0:1, 16:24]))
            release(3)

            v_n = vop(lambda h: h.tensor_scalar(row0[:], row0[:], -1.0, None, ALU.mult))
            wge("te", "ve", max(v_n, v_brow))
            mm(3, PA[3][:, 8:16], orow[:], brow[:], col=0, sem="te", stop=True)
            wge("ve", "te", C["te"])
            v_bb = vop(lambda h: h.tensor_copy(browb[:], PA[3][:, 8:16]))
            release(3)
            wge("te", "ve", v_n)
            for nb in range(8):
                mm(3, PA[3][:, nb:nb + 1], row1[0:1, nb * 128:(nb + 1) * 128],
                   o8[0:1, 0:1], col=0, sem="te" if nb == 7 else None)
            wge("ve", "te", C["te"])
            for nb in range(8):
                v_rcol = emit("ve", lambda h, b=nb: h.tensor_copy(
                    rcol[:, b:b + 1], PA[3][:, b:b + 1]),
                    "ve" if nb == 7 else None, 1)
            release(3)

            ncons = [0, 0]
            rcons = [0, 0]
            wge("te", "ve", v_rcol)
            for t in range(DT):
                sl = t % 2
                wge("sp", "ve", ncons[sl])
                wge("sp", "te", rcons[sl])
                vx = dma("sp", xst[:, sl, :], x2sp[t])
                wge("ve", "lsp", vx)
                wge("ve", "te", vb4)
                emit("ve", lambda h, s=sl: h.tensor_tensor(sqb[:], xst[:, s, :],
                                                           PA[0][:], ALU.mult))
                ncons[sl] = emit("ve", lambda h, t_=t: h.tensor_tensor(
                    xn1[:, t_, :], sqb[:], PA[1][:], ALU.subtract), "ve", 1)
                wge("te", "lsp", vx)
                for nb in range(8):
                    rcons[sl] = mm(2, PA[2][:, 64 + nb * 8: 72 + nb * 8],
                                   xst[:, sl, nb * 128:(nb + 1) * 128],
                                   rtw[:, t, :], col=0,
                                   sem="te" if nb == 7 else None)
            v_xn2 = C["ve"]
            release(0)
            release(1)
            wge("te", "ve", max(v_crw, v_brow, v_n))
            for nb in range(8):
                mm(2, PA[2][:, 64 + nb * 8: 72 + nb * 8],
                   row0[0:1, nb * 128:(nb + 1) * 128], crw[:], col=0,
                   sem="te" if nb == 7 else None)
            v_raw = C["te"]

            wge("ve", "te", v_raw)
            wge("ve", "ve", v_rcol)
            wge("ve", "ve", v_bb)
            for nb in range(8):
                reg = PA[2][:, 64 + nb * 8: 72 + nb * 8]
                vop(lambda h, P=reg, b=nb: h.scalar_tensor_tensor(
                    rsm[:, 0:8], P, rcol[:, b:b + 1], browb[:],
                    ALU.mult, ALU.add))
                vop(lambda h: h.tensor_reduce(rsm[:, 8:9], rsm[:, 0:8], AX.X, ALU.max))
                v_m = vop(lambda h: h.tensor_scalar(rsm[:, 9:10], rsm[:, 8:9], -1.0,
                                                    None, ALU.mult))
                wge("sc", "ve", v_m)
                vs = sop(lambda h: h.activation(rsm[:, 16:24], rsm[:, 0:8], AF.Exp,
                                                bias=rsm[:, 9:10],
                                                accum_out=rsm[:, 10:11]))
                wge("ve", "sc", vs)
                vop(lambda h: h.reciprocal(rsm[:, 11:12], rsm[:, 10:11]))
                vop(lambda h: h.tensor_scalar(rsm[:, 16:24], rsm[:, 16:24],
                                              rsm[:, 11:12], None, ALU.mult))
                vop(lambda h: h.tensor_reduce(rsm[:, 24:25], rsm[:, 16:24], AX.X,
                                              ALU.max))
                vop(lambda h: h.tensor_scalar(rsm[:, 32:40], rsm[:, 16:24],
                                              rsm[:, 24:25], 0.0, ALU.subtract,
                                              ALU.is_ge))
                vop(lambda h: h.tensor_scalar(rsm[:, 40:48], rsm[:, 32:40], -1e9,
                                              None, ALU.mult))
                vop(lambda h: h.tensor_tensor(rsm[:, 40:48], rsm[:, 16:24],
                                              rsm[:, 40:48], ALU.add))
                vop(lambda h: h.tensor_reduce(rsm[:, 25:26], rsm[:, 40:48], AX.X,
                                              ALU.max))
                vop(lambda h: h.tensor_scalar(rsm[:, 48:56], rsm[:, 40:48],
                                              rsm[:, 25:26], 0.0, ALU.subtract,
                                              ALU.is_ge))
                v_g2 = vop(lambda h: h.tensor_copy(rsm[:, 26:28], rsm[:, 24:26]))
                wge("sc", "ve", v_g2)
                vs = sop(lambda h: h.activation(rsm[:, 28:30], rsm[:, 26:28], AF.Exp,
                                                accum_out=rsm[:, 30:31]))
                wge("ve", "sc", vs)
                vop(lambda h: h.reciprocal(rsm[:, 31:32], rsm[:, 30:31]))
                vop(lambda h: h.tensor_scalar(rsm[:, 28:30], rsm[:, 28:30],
                                              rsm[:, 31:32], None, ALU.mult))
                vop(lambda h: h.tensor_scalar(rsm[:, 32:40], rsm[:, 32:40],
                                              rsm[:, 28:29], None, ALU.mult))
                vop(lambda h: h.tensor_scalar(rsm[:, 48:56], rsm[:, 48:56],
                                              rsm[:, 29:30], None, ALU.mult))
                v_w8 = vop(lambda h, b=nb: h.tensor_tensor(
                    wsb[:, b, :], rsm[:, 32:40], rsm[:, 48:56], ALU.add), sync=False)
            release(2)

            wge("te", "ve", v_w8)
            v_wt = 0
            for nb in range(8):
                wge("te", "ve", v_wt)
                mm(3, PA[3][0:8, 0:128], wsb[:, nb, :], idt[:], col=0,
                   sem="te", stop=True)
                wge("ve", "te", C["te"])
                v_wt = emit("ve", lambda h, b=nb: h.tensor_copy(
                    wTr[:, b * 128:(b + 1) * 128], PA[3][0:8, 0:128]), "ve", 1)
                release(3)
            wge("te", "ve", v_wt)
            mm(3, PA[3][0:1, 0:512], o8[:], wTr[:, 0:512], col=0)
            v_gs = mm(3, PA[3][0:1, 512:1024], o8[:], wTr[:, 512:1024], col=512,
                      sem="te", stop=True)
            wge("ve", "te", v_gs)
            v_gsr = vop(lambda h: h.tensor_copy(row1[:], PA[3][0:1, :]))
            release(3)

            wge("te", "ve", v_xn2)
            dcons = 0
            for lb in range(2):
                for eh in range(2):
                    wge("gp", "te", wrel[eh])
                    src = down_w[eh * 1024:(eh + 1) * 1024, lb * 512:(lb + 1) * 512]
                    dma("gp", wsl[eh], src.rearrange("(t p) d -> p t d", p=128))
                v_w = C["lgp"]
                wge("te", "lgp", v_w)
                wge("te", "sc", dcons)
                for ltr in range(4):
                    for eh in range(2):
                        for et in range(8):
                            for nh in range(2):
                                is_last = (eh == 1 and et == 7 and nh == 1)
                                wrel[eh] = mm(
                                    ltr, PA[ltr][:, nh * 512:(nh + 1) * 512],
                                    wsl[eh][:, et, ltr * 128:(ltr + 1) * 128],
                                    xn1[:, eh * 8 + et, nh * 512:(nh + 1) * 512],
                                    col=nh * 512, sem="te" if is_last else None)
                vte = C["te"]
                wge("sc", "te", vte)
                for ltr in range(4):
                    lt = lb * 4 + ltr
                    dcons = emit("sc", lambda h, p=ltr, l=lt: h.activation(
                        qsb[:, l, :], PA[p][:], AF.Gelu,
                        bias=colb[:, 16 + l:17 + l]), "sc" if ltr == 3 else None, 1)
                    release(ltr)
            v_gh = dcons

            v_te_done = C["te"]
            wge("sc", "te", v_te_done)
            for t in range(DT):
                v_gx = emit("sc", lambda h, t_=t: h.activation(
                    xn1[:, t_, :], xn1[:, t_, :], AF.Gelu),
                    "sc" if t == DT - 1 else None, 1)

            if upto == 7:
                wge("gp", "sc", v_gx)
                wge("gp", "ve", v_gsr)
                emit("gp", lambda h: h.dma_start(
                    out=outT[0:1024, :].rearrange("(t p) n -> p t n", p=128),
                    in_=qsb.ap()[:, 0:8, :]), "outp", 16)
                emit("gp", lambda h: h.dma_start(out=outT[1024:1032, :], in_=wTr[:]),
                     "outp", 16)
                emit("gp", lambda h: h.dma_start(out=outT[1032:1033, :], in_=row1[:]),
                     "outp", 16)
                emit("gp", lambda h: h.dma_start(out=outT[1040:1168, 0:8], in_=rcol[:]),
                     "outp", 16)
                wge("sp", "outp", 16 * 4)
                return

            HIDX = HID + [1024]
            moe_init = False
            v_gext = 0
            v_moe_ev = 0
            v_gr = 0
            for e in range(9):
                he = HIDX[e]
                groups = he // 1024
                shared = (e == 8)
                w1s = sh_w1 if shared else ew1[e]
                w2s = sh_w2 if shared else ew2[e]
                b1s = sh_b1 if shared else eb1[e]
                if not shared:
                    wge("gp", "te", C["te"])
                    wge("gp", "ve", v_gsr)
                    v_row = dma("gp", row0[0:1, :], wTr[e:e + 1, :])
                    wge("te", "ve", C["ve"])
                    wge("te", "lgp", v_row)
                    mm(2, PA[2][:, 0:512], orow[:], row0[0:1, 0:512], col=0)
                    v_bw = mm(2, PA[2][:, 512:1024], orow[:], row0[0:1, 512:1024],
                              col=512, sem="te", stop=True)
                    wge("ve", "te", v_bw)
                    v_bwc = vop(lambda h: h.tensor_copy(bwv[:], PA[2][:]))
                    release(2)
                wge("gp", "sc", C["sc"])
                vb1c = load_cols(b1cf[:, 0:groups * 8], b1s, groups * 8, pa=0)
                wge("sc", "ve", vb1c)
                v_gb1 = emit("sc", lambda h, g=groups: h.activation(
                    gb1c[:, 0:g * 8], b1cf[:, 0:g * 8], AF.Gelu), "sc", 1)
                for g in range(groups):
                    vws = [0, 0]
                    for lh in range(2):
                        wge("gp", "te", swrel[lh])
                        src = w1s[lh * 512:(lh + 1) * 512, g * 1024:(g + 1) * 1024]
                        vws[lh] = dma("gp", w1sl[lh],
                                      src.rearrange("(t p) d -> p t d", p=128))
                    vw2 = [0, 0]
                    for hh2 in range(2):
                        wge("gp", "te", kq_rel[hh2])
                        src = w2s[g * 1024 + hh2 * 512: g * 1024 + (hh2 + 1) * 512, :]
                        vw2[hh2] = dma("gp", w2sl[hh2],
                                       src.rearrange("(t p) d -> p t d", p=128))
                    if not shared:
                        wge("te", "lgp", vw2[1])
                        wge("te", "sc", v_gb1)
                        wge("te", "ve", v_gext)
                        for hh2 in range(2):
                            for htr in range(4):
                                htg = g * 8 + hh2 * 4 + htr
                                for nh in range(2):
                                    is_last = (hh2 == 1 and htr == 3 and nh == 1)
                                    mm(0, PA[0][0:1, nh * 512:(nh + 1) * 512],
                                       gb1c[:, htg:htg + 1],
                                       w2sl[hh2][:, htr, nh * 512:(nh + 1) * 512],
                                       col=nh * 512, sem="te" if is_last else None)
                        wge("ve", "te", C["te"])
                        wge("ve", "lgp", v_gr)
                        if g == 0:
                            v_gext = vop(lambda h: h.tensor_copy(row2[:], PA[0][0:1, :]),
                                         sync=False)
                        else:
                            v_gext = vop(lambda h: h.tensor_tensor(
                                row2[:], row2[:], PA[0][0:1, :], ALU.add), sync=False)
                        release(0)
                        wge("te", "ve", v_gext)
                    for nq in range(4):
                        no = nq * 256
                        for hh2 in range(2):
                            wge("te", "lgp", max(vws[1], vw2[1]))
                            wge("te", "sc", C["sc"])
                            for htr in range(4):
                                for lt8 in range(8):
                                    is_last = (htr == 3 and lt8 == 7)
                                    mm(hh2, PA[hh2][:, htr * 256:(htr + 1) * 256],
                                       w1sl[lt8 // 4][:, lt8 % 4,
                                                      (hh2 * 4 + htr) * 128:
                                                      (hh2 * 4 + htr + 1) * 128],
                                       qsb[:, lt8, no:no + 256],
                                       col=htr * 256, sem="te" if is_last else None)
                            vte = C["te"]
                            wge("sc", "te", vte)
                            wge("sc", "ve", v_moe_ev)
                            for htr in range(4):
                                htg = g * 8 + hh2 * 4 + htr
                                v_gel = emit("sc", lambda h, p=hh2, t_=htr, c2=htg:
                                             h.activation(
                                                 gtmp[:, p, t_ * 256:(t_ + 1) * 256],
                                                 PA[p][:, t_ * 256:(t_ + 1) * 256],
                                                 AF.Gelu, bias=b1cf[:, c2:c2 + 1]),
                                             "sc" if htr == 3 else None, 1)
                                release(hh2, htr * 256)
                            wge("ve", "sc", v_gel)
                            for htr in range(4):
                                sm = "ve" if htr == 3 else None
                                if shared:
                                    v_ga = emit("ve", lambda h, p=hh2, t_=htr:
                                                h.tensor_scalar(
                                                    ga[p][:, t_, :],
                                                    gtmp[:, p, t_ * 256:(t_ + 1) * 256],
                                                    0.1, None, ALU.mult), sm, 1)
                                else:
                                    v_ga = emit("ve", lambda h, p=hh2, t_=htr, n_=no:
                                                h.tensor_tensor(
                                                    ga[p][:, t_, :],
                                                    gtmp[:, p, t_ * 256:(t_ + 1) * 256],
                                                    bwv[:, n_:n_ + 256], ALU.mult),
                                                sm, 1)
                            wge("te", "ve", v_ga)
                        wge("te", "ve", v_moe_ev)
                        for lt8 in range(8):
                            pi = 2 + lt8 // 4
                            colo = (lt8 % 4) * 256
                            for hh2 in range(2):
                                for htr in range(4):
                                    is_last = (hh2 == 1 and htr == 3)
                                    mm(pi, PA[pi][:, colo:colo + 256],
                                       w2sl[hh2][:, htr, lt8 * 128:(lt8 + 1) * 128],
                                       ga[hh2][:, htr, :], col=colo,
                                       sem="te" if is_last else None)
                        vte = C["te"]
                        wge("ve", "te", vte)
                        for lt8 in range(8):
                            pi = 2 + lt8 // 4
                            colo = (lt8 % 4) * 256
                            sm = "ve" if lt8 == 7 else None
                            if not moe_init:
                                v_moe_ev = emit(
                                    "ve", lambda h, p=pi, c2=colo, l=lt8, n_=no:
                                    h.tensor_copy(moe[:, l, n_:n_ + 256],
                                                  PA[p][:, c2:c2 + 256]), sm, 1)
                            else:
                                v_moe_ev = emit(
                                    "ve", lambda h, p=pi, c2=colo, l=lt8, n_=no:
                                    h.scalar_tensor_tensor(
                                        moe[:, l, n_:n_ + 256], PA[p][:, c2:c2 + 256],
                                        1.0, moe[:, l, n_:n_ + 256],
                                        ALU.mult, ALU.add), sm, 1)
                            release(pi, colo)
                    moe_init = True
                    wrel[0] = wrel[1] = swrel[0] = swrel[1] = C["te"]
                if not shared:
                    v_ng = vop(lambda h: h.tensor_scalar(row2[:], row2[:], -1.0,
                                                         None, ALU.mult))
                    wge("gp", "ve", v_ng)
                    v_gr = dma("gp", gR[e:e + 1, :], row2[0:1, :])
                    v_gext = C["ve"]
            for lt8 in range(8):
                v_sb2 = vop(lambda h, l=lt8: h.tensor_scalar(
                    moe[:, l, :], moe[:, l, :], sb2c[:, l:l + 1], None, ALU.add),
                    sync=False)
            wge("ve", "ve", v_sb2)

            wge("te", "lgp", C["lgp"])
            wge("te", "ve", v_sb2)
            mm(0, PA[0][0:1, 0:512], o8[:], b2st[:, 0:512], col=0)
            mm(0, PA[0][0:1, 512:1024], o8[:], b2st[:, 512:1024], col=512)
            mm(0, PA[0][0:1, 0:512], n8[:], gR[:, 0:512], col=0)
            v_cr = mm(0, PA[0][0:1, 512:1024], n8[:], gR[:, 512:1024], col=512,
                      sem="te", stop=True)
            wge("ve", "te", v_cr)
            v_crow = vop(lambda h: h.tensor_copy(row0[:], PA[0][0:1, :]))
            release(0)
            wge("te", "ve", v_crow)
            for lt8 in range(8):
                for nq in range(4):
                    no = nq * 256
                    pi = 2 + lt8 // 4
                    colo = (lt8 % 4) * 256
                    mm(pi, PA[pi][:, colo:colo + 256],
                       row0[0:1, lt8 * 128:(lt8 + 1) * 128], row1[0:1, no:no + 256],
                       col=colo)
                    v_cm = mm(pi, PA[pi][:, colo:colo + 256],
                              gR[:, lt8 * 128:(lt8 + 1) * 128], wTr[:, no:no + 256],
                              col=colo, sem="te", stop=True)
                    wge("ve", "te", v_cm)
                    vc = emit("ve", lambda h, p=pi, c2=colo, l=lt8, n_=no:
                              h.scalar_tensor_tensor(
                                  moe[:, l, n_:n_ + 256], PA[p][:, c2:c2 + 256],
                                  1.0, moe[:, l, n_:n_ + 256], ALU.mult, ALU.add),
                              "ve", 1)
                    release(pi, colo)
                    wge("te", "ve", vc)
            v_moe_done = C["ve"]

            if upto == 8:
                wge("gp", "ve", v_moe_done)
                emit("gp", lambda h: h.dma_start(
                    out=outT[0:1024, :].rearrange("(t p) n -> p t n", p=128),
                    in_=moe.ap()), "outp", 16)
                wge("sp", "outp", 16)
                return

            wge("ve", "ve", v_moe_done)
            for lt8 in range(8):
                v_mbf = vop(lambda h, l=lt8: h.tensor_copy(moebf[:, l, :],
                                                           moe[:, l, :]), sync=False)
            wge("ve", "ve", v_mbf)
            wge("te", "ve", v_mbf)
            wge("te", "sc", v_gx)
            pcons = [0, 0]
            scons = [0, 0]
            fcons = [0, 0]
            up_rel = SV("s")
            co_rel = SV("b")
            for db in range(4):
                for lh in range(2):
                    wge("gp", "te", up_rel[lh])
                    src = up_w[lh * 512:(lh + 1) * 512, db * 512:(db + 1) * 512]
                    dma("gp", upsl[lh], src.rearrange("(t p) d -> p t d", p=128))
                for eh in range(2):
                    wge("gp", "te", co_rel[eh])
                    src = core_w[eh * 1024:(eh + 1) * 1024, db * 512:(db + 1) * 512]
                    dma("gp", cosl[eh], src.rearrange("(t p) d -> p t d", p=128))
                v_w = C["lgp"]
                wge("te", "lgp", v_w)
                for dtr in range(4):
                    dt = db * 4 + dtr
                    pi = dtr % 2
                    wge("te", "sc", pcons[pi])
                    for nh in range(2):
                        for lh in range(2):
                            for ltr in range(4):
                                up_rel[lh] = mm(
                                    pi, PA[pi][:, nh * 512:(nh + 1) * 512],
                                    upsl[lh][:, ltr, dtr * 128:(dtr + 1) * 128],
                                    moebf[:, lh * 4 + ltr, nh * 512:(nh + 1) * 512],
                                    col=nh * 512)
                        for eh in range(2):
                            for et in range(8):
                                is_last = (nh == 1 and eh == 1 and et == 7)
                                co_rel[eh] = mm(
                                    pi, PA[pi][:, nh * 512:(nh + 1) * 512],
                                    cosl[eh][:, et, dtr * 128:(dtr + 1) * 128],
                                    xn1[:, eh * 8 + et, nh * 512:(nh + 1) * 512],
                                    col=nh * 512, sem="te" if is_last else None)
                    vte = C["te"]
                    sl2 = dt % 2
                    wge("sc", "te", max(vte, v_sq_cons, scons[sl2]))
                    wge("sc", "lsp", fcons[sl2])
                    emit("sc", lambda h, p=pi, s=sl2, d=dt: h.activation(
                        xst[:, s, :], PA[p][:], AF.Identity,
                        bias=colb[:, 24 + d:25 + d]))
                    pcons[pi] = emit("sc", lambda h, s=sl2: h.activation(
                        sqb[:], xst[:, s, :], AF.Square), "sc", 1)
                    release(pi)
                    wge("te", "sc", pcons[pi])
                    mm(2, PA[2][0:1, 0:512], ocol[:], xst[:, sl2, 0:512], col=0)
                    mm(2, PA[2][0:1, 512:1024], ocol[:], xst[:, sl2, 512:1024], col=512)
                    mm(3, PA[3][0:1, 0:512], ocol[:], sqb[:, 0:512], col=0)
                    scons[sl2] = mm(3, PA[3][0:1, 512:1024], ocol[:], sqb[:, 512:1024],
                                    col=512, sem="te", stop=(dt == DT - 1))
                    v_sq_cons = scons[sl2]
                    wge("sp", "sc", pcons[pi])
                    fcons[sl2] = dma("sp", fsp[dt], xst[:, sl2, :])
            vb5 = ln_rows(C["te"])
            vcons = [0, 0]
            for t in range(DT):
                sl = t % 2
                wge("sp", "ve", vcons[sl])
                dma("sp", xst[:, sl, :], fsp[t])
                vx2 = dma("sp", rst[:, sl, :], x2sp[t])
                wge("ve", "lsp", vx2)
                wge("ve", "te", vb5)
                emit("ve", lambda h, s=sl: h.tensor_tensor(sqb[:], xst[:, s, :],
                                                           PA[0][:], ALU.mult))
                emit("ve", lambda h: h.tensor_tensor(sqb[:], sqb[:], PA[1][:],
                                                     ALU.subtract))
                vo = emit("ve", lambda h, s=sl: h.tensor_tensor(
                    rst[:, s, :], sqb[:], rst[:, s, :], ALU.add), "ve", 1)
                vcons[sl] = vo
                wge("sp", "ve", vo)
                emit("sp", lambda h, s=sl, t_=t: h.dma_start(
                    out=outT.rearrange("(a p) n -> a p n", p=128)[t_],
                    in_=rst[:, s, :]), "outp", 16)
            wge("sp", "outp", C["outp"])

        block.sync(lambda e: program("sp"))
        block.tensor(lambda e: program("te"))
        block.vector(lambda e: program("ve"))
        block.scalar(lambda e: program("sc"))
        block.gpsimd(lambda e: program("gp"))

    return nc


def prep(inputs):
    x = np.asarray(inputs["L0_attn_x"])
    xTfull = np.ascontiguousarray(x[0].T)
    base = {
        "wqT": np.ascontiguousarray(np.asarray(inputs["L0_attn_Wq"]).transpose(0, 2, 1)),
        "wkT": np.ascontiguousarray(np.asarray(inputs["L0_attn_Wk"]).transpose(0, 2, 1)),
        "wvT": np.ascontiguousarray(np.asarray(inputs["L0_attn_Wv"]).transpose(0, 2, 1)),
        "fcwT": np.ascontiguousarray(np.asarray(inputs["L0_attn_fc_w"]).T),
        "fc_b": np.asarray(inputs["L0_attn_fc_b"]),
        "down_w": np.asarray(inputs["L0_ffn_down_w"]),
        "down_b": np.asarray(inputs["L0_ffn_down_b"]),
        "up_w": np.asarray(inputs["L0_ffn_up_w"]),
        "up_b": np.asarray(inputs["L0_ffn_up_b"]),
        "router_w": np.asarray(inputs["L0_ffn_router_w"]),
        "router_b": np.asarray(inputs["L0_ffn_router_b"]),
        "sh_w1": np.asarray(inputs["L0_ffn_shared_w1"]),
        "sh_b1": np.asarray(inputs["L0_ffn_shared_b1"]),
        "sh_w2": np.asarray(inputs["L0_ffn_shared_w2"]),
        "sh_b2": np.asarray(inputs["L0_ffn_shared_b2"]),
        "core_w": np.asarray(inputs["L0_ffn_core_w"]),
        "core_b": np.asarray(inputs["L0_ffn_core_b"]),
    }
    for k in range(8):
        base[f"ew1_{k}"] = np.asarray(inputs[f"L0_ffn_e{k}_w1"])
        base[f"eb1_{k}"] = np.asarray(inputs[f"L0_ffn_e{k}_b1"])
        base[f"ew2_{k}"] = np.asarray(inputs[f"L0_ffn_e{k}_w2"])
        base[f"eb2_{k}"] = np.asarray(inputs[f"L0_ffn_e{k}_b2"])
    base = {k: np.ascontiguousarray(v, dtype=np.float32) for k, v in base.items()}
    in_maps = []
    for c in range(NC):
        m = dict(base)
        m["xT"] = np.ascontiguousarray(xTfull[:, c * NLOC:(c + 1) * NLOC])
        in_maps.append(m)
    return in_maps


def kernel(**inputs):
    in_maps = prep(inputs)
    nc = build()
    res = run_bass_kernel_spmd(nc, in_maps, core_ids=list(range(NC)))
    outTf = np.concatenate([r["outT"] for r in res.results], axis=1)
    return np.ascontiguousarray(outTf.T).reshape(1, 8192, 2048)
